# version 09 (final)
# baseline (speedup 1.0000x reference)
import contextlib
import numpy as np
import concourse.bass as bass
import concourse.mybir as mybir
from concourse.bass_utils import run_bass_kernel_spmd

F32 = mybir.dt.float32
BF16 = mybir.dt.bfloat16
AF = mybir.ActivationFunctionType
OP = mybir.AluOpType
AX = mybir.AxisListType

NCORES = 8
TOK = 1024
HALO = 8
TW = TOK + HALO
NCH = 8
DM = 2048
NH = 32
NPAIR = 16
DH = 5632
NKC = DM // 128
NJT = DH // 128
NEG = -1e30
EPSQ = 1e-8
NS = (344, 344, 344)
NSOFF = (0, 344, 688)
JBLK = 7
JBLOCKS = (7, 7, 7, 7, 7, 7, 2)


class Prog:
    def __init__(self, nc):
        self.nc = nc
        self.ops = {k: [] for k in ("sync", "tensor", "vector", "scalar", "gpsimd")}
        self.sems = {}
        self.counts = {}
        for name in ("qs", "qg", "qc", "b0", "b1", "t", "v", "a", "g", "x"):
            h = nc._stack.enter_context(nc.semaphore("sem_" + name))
            self.sems[name] = h
            self.counts[name] = 0

    def emit(self, eng, fn, waits=(), inc=None, inc_n=1):
        if inc is not None:
            self.counts[inc] += inc_n
        self.ops[eng].append((fn, tuple(waits), inc, inc_n))
        return (inc, self.counts[inc]) if inc else None

    def dma(self, queue, out, in_, waits=(), sem=None):
        inc = sem or {"sync": "qs", "gpsimd": "qg", "scalar": "qc"}[queue]
        self.emit(queue, lambda e, o=out, i=in_: e.dma_start(out=o, in_=i),
                  waits=waits, inc=inc, inc_n=16)
        return ((inc, self.counts[inc]),)

    def c(self, name):
        return (name, self.counts[name])

    def chain(self, e, inst):
        self.counts["x"] += 1
        inst.then_inc(self.sems["x"], 1)
        e.wait_ge(self.sems["x"], self.counts["x"])

    def allc(self):
        return tuple((k, v) for k, v in self.counts.items() if v > 0)

    def build(self, block):
        for eng_name in self.ops:
            def make(eng_name):
                def body(eng):
                    for fn, waits, inc, inc_n in self.ops[eng_name]:
                        for s, v in waits:
                            eng.wait_ge(self.sems[s], v)
                        inst = fn(eng)
                        if inc is not None:
                            inst.then_inc(self.sems[inc], inc_n)
                return body
            getattr(block, eng_name)(make(eng_name))


def host_constants():
    c = {}
    c["identc"] = np.eye(128, dtype=np.float32)
    s = np.arange(128)
    c["neg_triu"] = -np.triu(np.ones((128, 128), np.float32))
    m = np.where(s[None, :] >= s[:, None], 0.0, NEG).astype(np.float32)
    c["maskneg2"] = np.concatenate([m, m], axis=1)
    ps = np.zeros((32, NPAIR * 128), np.float32)
    for t in range(NPAIR):
        ps[2 * t, 128 * t:128 * t + 64] = 1.0
        ps[2 * t + 1, 128 * t + 64:128 * t + 128] = 1.0
    c["pairsel"] = ps
    c["negones"] = -np.ones((128, 1), np.float32)
    o9 = np.zeros((9, 32), np.float32)
    o9[8, :] = 1.0
    c["ones9"] = o9
    sel = np.zeros((2, 256), np.float32)
    sel[0, 0:128] = 1.0
    sel[1, 128:256] = 1.0
    c["rowsel"] = sel
    return c


def per_core_rhs9(i):
    k = np.arange(NCORES)
    maskrow = (k < i).astype(np.float32)
    inc = (k[:, None] <= k[None, :]).astype(np.float32)
    rhs9 = np.zeros((9, NCORES), np.float32)
    rhs9[0:8, :] = maskrow[:, None] - inc
    rhs9[8, :] = np.where(k >= i, NEG, 0.0)
    return rhs9


def build_program(stop_after="full", taps=()):
    nc = bass.Bass("TRN2", target_bir_lowering=False, debug=False,
                   detect_race_conditions=False)
    nc._stack = contextlib.ExitStack()
    ex = nc._stack

    def dram_in(name, shape, dt=F32):
        return nc.dram_tensor(name, list(shape), dt, kind="ExternalInput").ap()

    u_fm = dram_in("u_fm", (DM, TW))
    w_inT = dram_in("w_inT", (DM, 8224))
    w_outT = dram_in("w_outT", (DM, DM))
    w1T = dram_in("w1T", (DM, DH))
    w2T = dram_in("w2T", (DM, DH))
    w3T = dram_in("w3T", (DH, DM))
    conv_w_h = dram_in("conv_w", (6144, 4))
    conv_b_h = dram_in("conv_b", (6144, 1))
    zb_h = dram_in("z_bias_b", (128, DM))
    rms_h = dram_in("rms_b", (128, DM))
    drep_h = dram_in("d_rep", (128, NH))
    ident_h = dram_in("identc", (128, 128))
    negtr_h = dram_in("neg_triu", (128, 128))
    mneg_h = dram_in("maskneg2", (128, 256))
    psel_h = dram_in("pairsel", (32, NPAIR * 128))
    rhs9_h = dram_in("rhs9", (9, NCORES))
    nones_h = dram_in("negones", (128, 1))
    ones9_h = dram_in("ones9", (9, 32))
    rsel_h = dram_in("rowsel", (2, 256))

    out_fm = nc.dram_tensor("out_fm", [DM, TOK], F32, kind="ExternalOutput").ap()

    def dram_scr(name, shape, dt=F32):
        return nc.dram_tensor(name, list(shape), dt).ap()
    c_store = dram_scr("c_store", (NPAIR, 128, TOK))
    y_store = dram_scr("y_store", (NPAIR, 128, NCH * 128))
    sz_store = dram_scr("sz_store", (NKC, 128, NCH * 128))
    g_store = dram_scr("g_store", (NKC, 128, TOK))
    fp_store = dram_scr("fp_store", (NKC, 128, TOK))
    cc_in_t = nc.dram_tensor("cc_in", [128, 1056], F32)
    cc_out_t = nc.dram_tensor("cc_out", [NCORES * 128, 1056], F32)
    cc_out = cc_out_t.ap()

    taps_out = {}
    for tname, tshape in taps:
        taps_out[tname] = nc.dram_tensor("tap_" + tname, list(tshape), F32,
                                         kind="ExternalOutput").ap()

    p = Prog(nc)

    stacks = {k: contextlib.ExitStack() for k in ("s0", "s1", "s2", "s3")}

    def sbuf(name, shape, dt=F32, st=None):
        stk = ex if st is None else stacks[st]
        return stk.enter_context(nc.sbuf_tensor(name, list(shape), dt))

    xn_hi = sbuf("xn_hi", (128, NKC * TW), BF16)
    xn_lo = sbuf("xn_lo", (128, NKC * TW), BF16)
    states = sbuf("states", (128, NCH * 1024))
    acs = sbuf("acs", (128, NCH * 32))
    eacs = sbuf("eacs", (128, NCH * 32))
    ds_tm = sbuf("ds_tm", (128, NCH * 32))
    dlogrow = sbuf("dlogrow", (1, NCH * 32))
    dlogT = sbuf("dlogT", (32, NCH))
    prefT = sbuf("prefT", (32, NCH))
    dcolp = sbuf("dcolp", (128, NPAIR * NCH))
    pdcol = sbuf("pdcol", (128, NPAIR * NCH))
    coefc = sbuf("coefc", (128, NPAIR * NCORES))
    dlt9 = sbuf("dlt9", (9, 32))
    coefT = sbuf("coefT", (32, NCORES))
    sp_st = sbuf("sp_st", (128, 32))
    identc = sbuf("identc_s", (128, 128))
    neg_triu = sbuf("neg_triu_s", (128, 128))
    maskneg2 = sbuf("maskneg2_s", (128, 256))
    pairsel = sbuf("pairsel_s", (32, NPAIR * 128))
    rhs9s = sbuf("rhs9s", (9, NCORES))
    negones = sbuf("negones_s", (128, 1))
    rowsel = sbuf("rowsel_s", (2, 256))
    ones_row = sbuf("ones_row", (1, 128))
    ones_col = sbuf("ones_col", (128, 1))
    ones_col_b = sbuf("ones_col_b", (128, 1), BF16)
    d_rep = sbuf("d_rep_s", (128, NH))
    zb = sbuf("zb", (128, DM))
    conv_ws = sbuf("conv_ws", (128, 48 * 4))
    conv_bs = sbuf("conv_bs", (128, 48))
    th_col = sbuf("th_col", (128, 8))
    gam = sbuf("gam", (1, 16))
    wstage = sbuf("wstage", (128, NKC * 128))
    whi = sbuf("whi", (128, NKC * 128), BF16)
    wlo = sbuf("wlo", (128, NKC * 128), BF16)
    gst1 = sbuf("gst1", (128, 2048))
    gst2 = sbuf("gst2", (128, 2048))
    gst3 = sbuf("gst3", (128, 2048))

    PB = [ex.enter_context(nc.psum_tensor(f"pb{i}", [128, 512], F32)) for i in range(8)]

    p.dma("sync", identc[:, :], ident_h[:, :])
    p.dma("sync", neg_triu[:, :], negtr_h[:, :])
    p.dma("sync", maskneg2[:, :], mneg_h[:, :])
    p.dma("sync", pairsel[:, :], psel_h[:, :])
    p.dma("sync", rhs9s[:, :], rhs9_h[:, :])
    p.dma("sync", negones[:, :], nones_h[:, :])
    p.dma("sync", dlt9[:, :], ones9_h[:, :])
    p.dma("sync", rowsel[:, :], rsel_h[:, :])
    p.dma("sync", d_rep[:, :], drep_h[:, :])
    p.dma("sync", zb[:, :], zb_h[:, :])
    p.dma("sync", conv_ws[:, :].rearrange("p (j w) -> p j w", w=4),
          conv_w_h.rearrange("(j p) w -> p j w", p=128))
    p.ops["sync"].append((lambda e: e.dma_start(
        out=conv_bs[:, :].rearrange("p (j w) -> p j w", w=1),
        in_=conv_b_h.rearrange("(j p) w -> p j w", p=128),
        allow_slow_non_contiguous=True), (), "qs", 16))
    p.counts["qs"] += 16
    wconsts = ((("qs"), p.counts["qs"]),)

    def init_v(e):
        e.memset(ones_row[:, :], 1.0)
        e.memset(ones_col[:, :], 1.0)
        e.memset(ones_col_b[:, :], 1.0)
        e.memset(states[:, :], 0.0)
        return e.memset(prefT[:, 0:1], 0.0)
    p.emit("vector", init_v, waits=wconsts, inc="v")

    ustage = sbuf("ustage", (128, 2 * TW), st="s0")
    stat = sbuf("stat", (1, 2 * TW), st="s0")

    if "dlt9e" in taps_out:
        p.dma("gpsimd", taps_out["dlt9e"][:, :], dlt9[:, :], waits=[p.c("qs")])

    for k in range(NKC):
        half = k % 2
        buf = ustage[:, half * TW:half * TW + TW]
        w = [p.c("t")] if k >= 2 else [p.c("v")]
        wld = p.dma("sync", buf, u_fm[128 * k:128 * k + 128, :], waits=w,
                    sem=f"b{half}")
        ca = p.emit("scalar", lambda e, b=buf: e.activation(gst1[:, 0:TW], b, AF.Square),
                    waits=list(wld) + [p.c("t")], inc="a")
        for si in range(3):
            off, n = NSOFF[si], NS[si]
            p.emit("tensor", lambda e, b=buf, o=off, n=n, k=k, si=si:
                   e.matmul(PB[si][0:1, 0:n], ones_col[:, :], b[:, o:o + n],
                            start=(k == 0), stop=(k == NKC - 1)),
                   waits=list(wld))
        for si in range(3):
            off, n = NSOFF[si], NS[si]
            p.emit("tensor", lambda e, o=off, n=n, k=k, si=si:
                   e.matmul(PB[3 + si][0:1, 0:n], ones_col[:, :], gst1[:, o:o + n],
                            start=(k == 0), stop=(k == NKC - 1)),
                   waits=[ca], inc=("t" if si == 2 else None))
    ct_stats = p.c("t")

    def fin_stats(e):
        for si in range(3):
            off, n = NSOFF[si], NS[si]
            e.tensor_scalar(stat[0:1, off:off + n], PB[si][0:1, 0:n], 1.0 / DM, 0.0,
                            OP.mult, OP.add)
            e.tensor_scalar(stat[0:1, TW + off:TW + off + n], PB[3 + si][0:1, 0:n],
                            1.0 / DM, 0.0, OP.mult, OP.add)
        m = stat[0:1, 0:TW]
        var = stat[0:1, TW:2 * TW]
        msq = ustage[0:1, 0:TW]
        i1 = e.tensor_tensor(msq, m, m, OP.mult)
        p.chain(e, i1)
        i2 = e.tensor_tensor(var, var, msq, OP.subtract)
        p.chain(e, i2)
        i3 = e.tensor_scalar(var, var, 1e-5, 0.0, OP.add, OP.add)
        p.chain(e, i3)
        return e.reciprocal(ustage[0:1, TW:2 * TW], var)
    cv = p.emit("vector", fin_stats, waits=[ct_stats], inc="v")
    ca = p.emit("scalar", lambda e: e.activation(ustage[0:1, 0:TW],
                ustage[0:1, TW:2 * TW], AF.Sqrt), waits=[cv], inc="a")

    def newton(e):
        y = ustage[0:1, 0:TW]
        var = stat[0:1, TW:2 * TW]
        t = gst1[0:1, 0:TW]
        i1 = e.tensor_tensor(t, y, y, OP.mult)
        p.chain(e, i1)
        i2 = e.tensor_tensor(t, t, var, OP.mult)
        p.chain(e, i2)
        i3 = e.tensor_scalar(t, t, -0.5, 1.5, OP.mult, OP.add)
        p.chain(e, i3)
        return e.tensor_tensor(y, y, t, OP.mult)
    cv = p.emit("vector", newton, waits=[ca], inc="v")
    for si in range(3):
        off, n = NSOFF[si], NS[si]
        p.emit("tensor", lambda e, o=off, n=n, si=si:
               e.matmul(PB[3 + si][:, 0:n], ones_row[0:1, 0:128], stat[0:1, o:o + n],
                        start=True, stop=True), waits=[cv])
        p.emit("tensor", lambda e, o=off, n=n, si=si:
               e.matmul(PB[si][:, 0:n], ones_row[0:1, 0:128], ustage[0:1, o:o + n],
                        start=True, stop=True),
               waits=[cv], inc=("t" if si == 2 else None))
    ct_bc = p.c("t")

    cv_xn = None
    for k in range(NKC):
        half = k % 2
        buf = ustage[:, half * TW:half * TW + TW]
        w = [p.c("v")] if k >= 2 else [ct_bc]
        wld = p.dma("sync", buf, u_fm[128 * k:128 * k + 128, :], waits=w,
                    sem=f"b{half}")

        def mk_xn(e, b=buf, k=k):
            t = gst1[:, 0:TW]
            for si in range(3):
                off, n = NSOFF[si], NS[si]
                e.tensor_tensor(t[:, off:off + n], b[:, off:off + n],
                                PB[3 + si][:, 0:n], OP.subtract)
                e.tensor_tensor(t[:, off:off + n], t[:, off:off + n],
                                PB[si][:, 0:n], OP.mult)
            e.tensor_copy(xn_hi[:, k * TW:(k + 1) * TW], t)
            return e.scalar_tensor_tensor(xn_lo[:, k * TW:(k + 1) * TW], t, 1.0,
                                          xn_hi[:, k * TW:(k + 1) * TW],
                                          OP.mult, OP.subtract)
        cv_xn = p.emit("vector", mk_xn, waits=list(wld) + [ct_bc], inc="v")

    if "xn" in taps_out:
        for k in range(NKC):
            cvt = p.emit("vector", lambda e, k=k: e.tensor_tensor(gst2[:, 0:TW],
                         xn_hi[:, k * TW:(k + 1) * TW], xn_lo[:, k * TW:(k + 1) * TW],
                         OP.add), waits=[cv_xn, p.c("qg")], inc="v")
            p.dma("gpsimd", taps_out["xn"][128 * k:128 * k + 128, :], gst2[:, 0:TW],
                  waits=[cvt])

    gam_srcs = [(w1T, DM, DH), (w2T, DM, DH), (w3T, DH, DM)]
    prev_abs = {0: None, 1: None}
    prev_mm = {0: None, 1: None}
    for wi, (wt, R, Cw) in enumerate(gam_srcs):
        nrow = R // 128
        cblocks = [(c0, min(1024, Cw - c0)) for c0 in range(0, Cw, 1024)]
        nblk = nrow * len(cblocks)
        bi = 0
        for r in range(nrow):
            for (c0, cw) in cblocks:
                half = bi % 2
                buf = ustage[:, half * TW:half * TW + cw]
                w = ([prev_abs[half]] if prev_abs[half] else [cv_xn])
                wld = p.dma("sync", buf, wt[128 * r:128 * r + 128, c0:c0 + cw],
                            waits=w, sem=f"b{half}")
                wq_ = [prev_mm[half]] if prev_mm[half] else []
                prev_abs[half] = p.emit("scalar", lambda e, b=buf, h=half, cw=cw:
                                        e.activation(whi[:, h * 1024:h * 1024 + cw],
                                                     b, AF.Abs),
                                        waits=list(wld) + wq_, inc="a")
                last = (bi == nblk - 1)
                for s0 in range(0, cw, 512):
                    prev_mm[half] = p.emit("tensor", lambda e, h=half, s0=s0,
                           st=(bi == 0 and s0 == 0 and wi == 0 if False else (bi == 0 and s0 == 0)),
                           sp=(last and s0 + 512 >= cw):
                           e.matmul(PB[6][0:1, 0:512], ones_col_b[:, :],
                                    whi[:, h * 1024 + s0:h * 1024 + s0 + 512],
                                    start=st, stop=sp),
                           waits=[prev_abs[half]], inc="t")
                bi += 1
        ctw = p.c("t")
        p.emit("vector", lambda e, wi=wi: e.tensor_reduce(gam[0:1, wi:wi + 1],
               PB[6][0:1, 0:512], axis=AX.X, op=OP.add), waits=[ctw], inc="v")

    def mk_th(e):
        i1 = e.tensor_scalar(gam[0:1, 4:7], gam[0:1, 0:3], 1.0 / (DH * DM), EPSQ,
                             OP.mult, OP.add)
        p.chain(e, i1)
        e.tensor_scalar(gam[0:1, 8:11], gam[0:1, 4:7], 0.5, 0.0, OP.mult, OP.add)
        return e.tensor_scalar(gam[0:1, 11:14], gam[0:1, 4:7], -0.5, 0.0,
                               OP.mult, OP.add)
    cv = p.emit("vector", mk_th, waits=[p.c("v")], inc="v")
    ct = p.emit("tensor", lambda e: e.matmul(PB[6][0:128, 0:6], ones_row[0:1, 0:128],
                gam[0:1, 8:14], start=True, stop=True), waits=[cv], inc="t")
    cth = p.emit("vector", lambda e: e.tensor_copy(th_col[:, 0:6], PB[6][0:128, 0:6]),
                 waits=[ct], inc="v")
    if "gam" in taps_out:
        p.dma("gpsimd", taps_out["gam"][0:1, :], gam[0:1, 0:16], waits=[cth])

    if stop_after == "p0":
        if "dlt9e" in taps_out:
            p.dma("gpsimd", taps_out["dlt9e"][:, :], dlt9[:, :], waits=[p.c("qs")])
        p.emit("gpsimd", lambda e: e.engine_nop(), waits=p.allc())
        with nc.Block() as block:
            p.build(block)
        return nc, taps_out

    stacks["s0"].close()
    xbc1 = sbuf("xbc1", (128, TW), st="s1")
    xbc2 = sbuf("xbc2", (128, TW), st="s1")
    bsil = sbuf("bsil", (128, TW), st="s1")
    csil = sbuf("csil", (128, TW), st="s1")
    x_tm = sbuf("x_tm", (128, NCH * 128), st="s1")
    bd_tmS = sbuf("bd_tmS", (128, 128), st="s1")
    arow = sbuf("arow", (2, 128), st="s1")
    t2s = sbuf("t2s", (128, 256), st="s1")
    es = sbuf("es", (128, 256), st="s1")
    mts = sbuf("mts", (128, 256), st="s1")
    ytmp = sbuf("ytmp", (128, 128), st="s1")
    ypair = sbuf("ypair", (128, NCH * 128), st="s1")
    szst = sbuf("szst", (128, NCH * 128), st="s1")

    lw_state = {"spl": None}

    def load_win_tile(ecol0, ncols):
        src = w_inT.rearrange("(k p) e -> p k e", p=128)[:, :, ecol0:ecol0 + ncols]
        w = list(lw_state["spl"]) if lw_state["spl"] else [p.c("t"), p.c("a"), p.c("v")]
        wl = p.dma("sync", wstage[:, 0:NKC * ncols].rearrange("p (k e) -> p k e", e=ncols),
                   src, waits=w)
        ca = p.emit("scalar", lambda e, n=ncols: e.activation(whi[:, 0:NKC * n],
                    wstage[:, 0:NKC * n], AF.Copy),
                    waits=list(wl) + [p.c("t")], inc="a")
        cv = p.emit("vector", lambda e, n=ncols: e.scalar_tensor_tensor(
                    wlo[:, 0:NKC * n], wstage[:, 0:NKC * n], 1.0, whi[:, 0:NKC * n],
                    OP.mult, OP.subtract), waits=[ca], inc="v")
        lw_state["spl"] = (ca, cv)
        return cv

    def proj_fm(cw_v, extra=()):
        first = True
        for si in range(3):
            off, n = NSOFF[si], NS[si]
            idx = 0
            for k in range(NKC):
                for (wt_, xt_) in ((whi, xn_hi), (whi, xn_lo), (wlo, xn_hi)):
                    st, sp = idx == 0, idx == 3 * NKC - 1
                    p.emit("tensor", lambda e, k=k, w_=wt_, x_=xt_, o=off, n=n,
                           st=st, sp=sp, si=si:
                           e.matmul(PB[si][:, 0:n], w_[:, 128 * k:128 * (k + 1)],
                                    x_[:, k * TW + o:k * TW + o + n],
                                    start=st, stop=sp),
                           waits=([cw_v] + [w for w in extra if w]) if first else (),
                           inc=("t" if sp and si == 2 else None))
                    first = False
                    idx += 1
        return p.c("t")

    def evac_proj(dst, ct, extra=()):
        def f(e):
            last = None
            for si in range(3):
                off, n = NSOFF[si], NS[si]
                last = e.activation(dst[:, off:off + n], PB[si][:, 0:n], AF.Copy)
            return last
        return p.emit("scalar", f, waits=[ct] + [w for w in extra if w], inc="a")

    def conv_tile(jt, src, dst, waits):
        def f(e):
            w0 = conv_ws[:, 4 * jt + 0:4 * jt + 1]
            b0 = conv_bs[:, jt:jt + 1]
            e.tensor_scalar(dst[:, 0:TOK], src[:, 5:5 + TOK], w0, b0, OP.mult, OP.add)
            last = None
            for w in range(1, 4):
                wc = conv_ws[:, 4 * jt + w:4 * jt + w + 1]
                last = e.scalar_tensor_tensor(dst[:, 0:TOK], src[:, 5 + w:5 + w + TOK],
                                              wc, dst[:, 0:TOK], OP.mult, OP.add)
            return last
        return p.emit("vector", f, waits=waits, inc="v")

    cwv = load_win_tile(8192, 32)
    for c in range(NCH):
        idx = 0
        for k in range(NKC):
            for (wt_, xt_) in ((whi, xn_hi), (whi, xn_lo), (wlo, xn_hi)):
                st, sp = idx == 0, idx == 3 * NKC - 1
                p.emit("tensor", lambda e, k=k, w_=wt_, x_=xt_, c=c, st=st, sp=sp:
                       e.matmul(PB[7][:, 0:32],
                                x_[:, k * TW + HALO + 128 * c:k * TW + HALO + 128 * (c + 1)],
                                w_[:, 32 * k:32 * (k + 1)], start=st, stop=sp),
                       waits=[cwv, p.c("a"), p.c("v")] if idx == 0 else (),
                       inc=("t" if sp else None))
                idx += 1
        ct = p.c("t")
        ca = p.emit("scalar", lambda e: e.activation(gst1[:, 0:32], PB[7][:, 0:32],
                    AF.Exp), waits=[ct], inc="a")
        ca = p.emit("scalar", lambda e: e.activation(sp_st[:, 0:32], gst1[:, 0:32],
                    AF.Ln, bias=1.0), waits=[ca], inc="a")
        ct = p.emit("tensor", lambda e: e.matmul(PB[7][:, 32:64], neg_triu[:, :],
                    sp_st[:, 0:32], start=True, stop=True), waits=[ca], inc="t")
        ca = p.emit("scalar", lambda e, c=c: e.activation(acs[:, 32 * c:32 * (c + 1)],
                    PB[7][:, 32:64], AF.Copy), waits=[ct], inc="a")
        ca = p.emit("scalar", lambda e, c=c: e.activation(eacs[:, 32 * c:32 * (c + 1)],
                    PB[7][:, 32:64], AF.Exp), waits=[ca], inc="a")
        ct = p.emit("tensor", lambda e, c=c: e.matmul(PB[6][0:32, 8 + c:9 + c],
                    sp_st[:, 0:32], negones[:, :], start=True, stop=True),
                    waits=[ca], inc="t")
        cv = p.emit("vector", lambda e, c=c: e.tensor_copy(dlogT[:, c:c + 1],
                    PB[6][0:32, 8 + c:9 + c]), waits=[ct], inc="v")
        ct = p.emit("tensor", lambda e, c=c: e.transpose(PB[6][0:1, 40:72],
                    dlogT[:, c:c + 1], identc[0:32, 0:32]), waits=[cv], inc="t")
        cv = p.emit("vector", lambda e, c=c: e.tensor_copy(
                    dlogrow[0:1, 32 * c:32 * (c + 1)], PB[6][0:1, 40:72]),
                    waits=[ct], inc="v")
        ct = p.emit("tensor", lambda e, c=c: e.matmul(PB[6][:, 40:72],
                    ones_row[0:1, 0:128], dlogrow[0:1, 32 * c:32 * (c + 1)],
                    start=True, stop=True), waits=[cv], inc="t")
        cv = p.emit("vector", lambda e, c=c: e.tensor_tensor(gst1[:, 0:32],
                    PB[6][:, 40:72], acs[:, 32 * c:32 * (c + 1)], OP.subtract),
                    waits=[ct], inc="v")
        ca = p.emit("scalar", lambda e, c=c: e.activation(ds_tm[:, 32 * c:32 * (c + 1)],
                    gst1[:, 0:32], AF.Exp), waits=[cv], inc="a")

    ca = p.emit("scalar", lambda e: e.activation(gst1[0:32, 0:8], dlogT[:, :], AF.Exp),
                waits=[p.c("v"), p.c("a")], inc="a")

    def mk_pref(e):
        i = e.memset(prefT[:, 0:1], 0.0)
        for c in range(1, NCH):
            p.chain(e, i)
            i = e.tensor_tensor(prefT[:, c:c + 1], prefT[:, c - 1:c],
                                dlogT[:, c - 1:c], OP.add)
        return i
    cv = p.emit("vector", mk_pref, waits=[ca], inc="v")
    ca = p.emit("scalar", lambda e: e.activation(gst1[0:32, 8:16], prefT[:, :], AF.Exp),
                waits=[cv], inc="a")
    for t in range(NPAIR):
        bk = PB[6] if t % 2 == 0 else PB[7]
        ct = p.emit("tensor", lambda e, t=t, bk=bk: e.matmul(bk[:, 0:8],
                    pairsel[:, 128 * t:128 * (t + 1)], gst1[0:32, 0:8],
                    start=True, stop=True), waits=[ca, p.c("v")], inc="t")
        p.emit("vector", lambda e, t=t, bk=bk: e.tensor_copy(dcolp[:, 8 * t:8 * (t + 1)],
               bk[:, 0:8]), waits=[ct], inc="v")
    for t in range(NPAIR):
        bk = PB[6] if t % 2 == 0 else PB[7]
        ct = p.emit("tensor", lambda e, t=t, bk=bk: e.matmul(bk[:, 0:8],
                    pairsel[:, 128 * t:128 * (t + 1)], gst1[0:32, 8:16],
                    start=True, stop=True), waits=[ca, p.c("v")], inc="t")
        p.emit("vector", lambda e, t=t, bk=bk: e.tensor_copy(pdcol[:, 8 * t:8 * (t + 1)],
               bk[:, 0:8]), waits=[ct], inc="v")
    cv_cols = p.c("v")

    if stop_after == "p1a":
        for nm, src, wd in (("acs", acs, NCH * 32), ("ds", ds_tm, NCH * 32),
                            ("eacs", eacs, NCH * 32), ("dcolp", dcolp, 128),
                            ("pdcol", pdcol, 128), ("dlogrow", dlogrow, 256)):
            if nm in taps_out:
                p.dma("gpsimd", taps_out[nm][:, :],
                      src[0:taps_out[nm].shape[0], 0:taps_out[nm].shape[1]],
                      waits=[cv_cols, p.c("a")])
        p.emit("gpsimd", lambda e: e.engine_nop(), waits=p.allc())
        with nc.Block() as block:
            p.build(block)
        return nc, taps_out

    z_tt = {0: None, 1: None}
    z_silu = None
    for j in range(NKC):
        cwv = load_win_tile(6144 + 128 * j, 128)
        for c in range(NCH):
            bank = PB[c % 2]
            idx = 0
            for k in range(NKC):
                for (wt_, xt_) in ((whi, xn_hi), (whi, xn_lo), (wlo, xn_hi)):
                    st, sp = idx == 0, idx == 3 * NKC - 1
                    wv = ()
                    if idx == 0:
                        wv = [cwv] + ([z_tt[c % 2]] if z_tt[c % 2]
                                      else [p.c("v"), p.c("a")])
                    p.emit("tensor", lambda e, k=k, w_=wt_, x_=xt_, c=c, st=st,
                           sp=sp, b=bank:
                           e.matmul(b[:, 0:128],
                                    x_[:, k * TW + HALO + 128 * c:k * TW + HALO + 128 * (c + 1)],
                                    w_[:, 128 * k:128 * (k + 1)], start=st, stop=sp),
                           waits=wv, inc=("t" if sp else None))
                    idx += 1
            ct = p.c("t")
            cv = p.emit("vector", lambda e, j=j, b=bank: e.tensor_tensor(
                        gst2[:, 0:128], b[:, 0:128], zb[:, 128 * j:128 * (j + 1)],
                        OP.add),
                        waits=[ct] + ([z_silu] if z_silu else [p.c("a")]), inc="v")
            z_tt[c % 2] = cv
            z_silu = p.emit("scalar", lambda e, c=c: e.activation(
                            szst[:, 128 * c:128 * (c + 1)], gst2[:, 0:128], AF.Silu),
                            waits=[cv] + ([p.c("qg")] if c == 0 else []), inc="a")
        p.dma("gpsimd", sz_store[j, :, :], szst[:, :], waits=[p.c("a")])
    cqg_sz = p.c("qg")

    if stop_after == "p1z":
        p.emit("gpsimd", lambda e: e.engine_nop(), waits=p.allc())
        with nc.Block() as block:
            p.build(block)
        return nc, taps_out

    ssd_prev = {k: None for k in ("ca_st", "t_T", "cv_t2", "cv_mt", "ca_exp",
                                  "cv_bd", "t_st", "t_yd", "ca_y", "cv_y")}
    prev_evac = None
    prev_ccv = None
    for j in range(NPAIR):
        cwv = load_win_tile(128 * j, 128)
        xtra = ([prev_evac, ssd_prev["cv_mt"]] if prev_evac
                else [p.c("v"), p.c("a")])
        ct = proj_fm(cwv, extra=xtra)
        ca_xev = evac_proj(xbc1, ct, extra=[prev_ccv] if prev_ccv else [])
        cwvB = load_win_tile(2048 + 128 * j, 128)
        cv_xcv = conv_tile(j, xbc1, xbc2, [ca_xev])
        ctB = proj_fm(cwvB, extra=[ca_xev])
        ca_xsil = p.emit("scalar", lambda e: e.activation(xbc2[:, 0:TOK],
                         xbc2[:, 0:TOK], AF.Silu), waits=[cv_xcv], inc="a")
        ca_bev = evac_proj(xbc1, ctB, extra=[cv_xcv])
        cwvC = load_win_tile(4096 + 128 * j, 128)
        cv_bcv = conv_tile(16 + j, xbc1, bsil,
                           [ca_bev] + ([ssd_prev["t_st"]] if ssd_prev["t_st"] else []))
        ctC = proj_fm(cwvC, extra=[ca_bev])
        cv_prevtr = None
        for c in range(NCH):
            wtr = ([ca_xsil, p.c("v")] +
                   ([ssd_prev["ca_st"]] if ssd_prev["ca_st"] else [])) if c == 0                   else [cv_prevtr]
            ctt = p.emit("tensor", lambda e, c=c: e.transpose(PB[3][:, 0:128],
                         xbc2[:, 128 * c:128 * (c + 1)], identc[:, :]),
                         waits=wtr, inc="t")
            cv_prevtr = p.emit("vector", lambda e, c=c: e.tensor_copy(
                               x_tm[:, 128 * c:128 * (c + 1)], PB[3][:, 0:128]),
                               waits=[ctt], inc="v")
        cv_xtm = cv_prevtr
        ca_bsil = p.emit("scalar", lambda e: e.activation(bsil[:, 0:TOK],
                         bsil[:, 0:TOK], AF.Silu), waits=[cv_bcv], inc="a")
        ca_cev = evac_proj(xbc1, ctC, extra=[cv_bcv])
        prev_evac = ca_cev
        cv_ccv = conv_tile(32 + j, xbc1, csil,
                           [ca_cev, p.c("qg")] +
                           ([ssd_prev["t_st"]] if ssd_prev["t_st"] else []))
        prev_ccv = cv_ccv
        ca_csil = p.emit("scalar", lambda e: e.activation(csil[:, 0:TOK],
                         csil[:, 0:TOK], AF.Silu), waits=[cv_ccv], inc="a")
        p.dma("gpsimd", c_store[j, :, :], csil[:, 0:TOK], waits=[ca_csil])

        pv = ssd_prev
        for c in range(NCH):
            ct_ar = p.emit("tensor", lambda e, c=c, j=j: e.transpose(PB[3][0:2, 0:128],
                           acs[:, 32 * c + 2 * j:32 * c + 2 * j + 2], identc[:, :]),
                           waits=(([pv["ca_st"]] if pv["ca_st"] else [ca_csil]) +
                                  ([cv_xtm] if c == 0 else [])),
                           inc="t")
            cv_ar = p.emit("vector", lambda e: e.tensor_copy(arow[:, :], PB[3][0:2, 0:128]),
                           waits=[ct_ar] + ([pv["t_T"]] if pv["t_T"] else []), inc="v")
            p.emit("tensor", lambda e: e.matmul(PB[5][:, 0:128], rowsel[:, 0:128],
                   arow[:, :], start=True, stop=True),
                   waits=[cv_ar] + ([pv["cv_t2"]] if pv["cv_t2"] else []))
            ct_T = p.emit("tensor", lambda e: e.matmul(PB[5][:, 128:256],
                          rowsel[:, 128:256], arow[:, :], start=True, stop=True),
                          inc="t")
            p.emit("tensor", lambda e, c=c: e.matmul(PB[4][:, 0:128],
                   bsil[0:64, 128 * c:128 * (c + 1)], csil[0:64, 128 * c:128 * (c + 1)],
                   start=True, stop=True),
                   waits=[ca_csil] + ([pv["cv_mt"]] if pv["cv_mt"] else []))
            ct_sc = p.emit("tensor", lambda e, c=c: e.matmul(PB[0][:, 0:128],
                           bsil[64:128, 128 * c:128 * (c + 1)],
                           csil[64:128, 128 * c:128 * (c + 1)], start=True, stop=True),
                           inc="t")
            ct_btr = p.emit("tensor", lambda e, c=c: e.transpose(PB[6][:, 0:128],
                            bsil[:, 128 * c:128 * (c + 1)], identc[:, :]),
                            waits=[pv["cv_bd"]] if pv["cv_bd"] else [], inc="t")

            def mk_t2(e, c=c, j=j):
                e.scalar_tensor_tensor(t2s[:, 0:128], PB[5][:, 0:128],
                                       acs[:, 32 * c + 2 * j:32 * c + 2 * j + 1],
                                       maskneg2[:, 0:128], OP.subtract, OP.add)
                return e.scalar_tensor_tensor(t2s[:, 128:256], PB[5][:, 128:256],
                                              acs[:, 32 * c + 2 * j + 1:32 * c + 2 * j + 2],
                                              maskneg2[:, 128:256], OP.subtract, OP.add)
            cv_t2 = p.emit("vector", mk_t2,
                           waits=[ct_T] + ([pv["ca_exp"]] if pv["ca_exp"] else []),
                           inc="v")
            ca_exp = p.emit("scalar", lambda e: e.activation(es[:, :], t2s[:, :], AF.Exp),
                            waits=[cv_t2] + ([pv["cv_mt"]] if pv["cv_mt"] else []),
                            inc="a")

            def mk_mt(e):
                e.tensor_tensor(mts[:, 0:128], es[:, 0:128], PB[4][:, 0:128], OP.mult)
                return e.tensor_tensor(mts[:, 128:256], es[:, 128:256],
                                       PB[0][:, 0:128], OP.mult)
            cv_mt = p.emit("vector", mk_mt,
                           waits=[ca_exp, ct_sc] + ([pv["t_yd"]] if pv["t_yd"] else []),
                           inc="v")

            def mk_bd(e, c=c, j=j):
                e.tensor_scalar(bd_tmS[:, 0:64], PB[6][:, 0:64],
                                ds_tm[:, 32 * c + 2 * j:32 * c + 2 * j + 1], 0.0,
                                OP.mult, OP.add)
                return e.tensor_scalar(bd_tmS[:, 64:128], PB[6][:, 64:128],
                                       ds_tm[:, 32 * c + 2 * j + 1:32 * c + 2 * j + 2],
                                       0.0, OP.mult, OP.add)
            cv_bd = p.emit("vector", mk_bd,
                           waits=[ct_btr] + ([pv["t_st"]] if pv["t_st"] else []),
                           inc="v")
            p.emit("tensor", lambda e, c=c: e.matmul(PB[3][0:64, 0:64],
                   bd_tmS[:, 0:64], x_tm[:, 128 * c:128 * c + 64],
                   start=True, stop=True), waits=[cv_bd, cv_ar])
            ct_st = p.emit("tensor", lambda e, c=c: e.matmul(PB[3][64:128, 0:64],
                           bd_tmS[:, 64:128], x_tm[:, 128 * c + 64:128 * c + 128],
                           start=True, stop=True), inc="t")
            ca_st = p.emit("scalar", lambda e, c=c, j=j: e.activation(
                           states[:, 1024 * c + 64 * j:1024 * c + 64 * (j + 1)],
                           PB[3][:, 0:64], AF.Copy), waits=[ct_st], inc="a")
            p.emit("tensor", lambda e, c=c: e.matmul(PB[7][:, 0:64],
                   mts[:, 0:128], x_tm[:, 128 * c:128 * c + 64],
                   start=True, stop=True),
                   waits=[cv_mt] + ([pv["ca_y"]] if pv["ca_y"] else []))
            ct_yd = p.emit("tensor", lambda e, c=c: e.matmul(PB[7][:, 64:128],
                           mts[:, 128:256], x_tm[:, 128 * c + 64:128 * c + 128],
                           start=True, stop=True), inc="t")
            ca_y = p.emit("scalar", lambda e: e.activation(ytmp[:, :], PB[7][:, 0:128],
                          AF.Copy),
                          waits=[ct_yd] + ([pv["cv_y"]] if pv["cv_y"] else []),
                          inc="a")

            def mk_y(e, c=c, j=j):
                e.scalar_tensor_tensor(ypair[:, 128 * c:128 * c + 64],
                                       x_tm[:, 128 * c:128 * c + 64],
                                       d_rep[:, 2 * j:2 * j + 1], ytmp[:, 0:64],
                                       OP.mult, OP.add)
                return e.scalar_tensor_tensor(ypair[:, 128 * c + 64:128 * c + 128],
                                              x_tm[:, 128 * c + 64:128 * c + 128],
                                              d_rep[:, 2 * j + 1:2 * j + 2],
                                              ytmp[:, 64:128], OP.mult, OP.add)
            cv_y = p.emit("vector", mk_y, waits=[ca_y] + ([p.c("qg")] if c == 0 else []),
                          inc="v")
            pv = {"ca_st": ca_st, "t_T": ct_T, "cv_t2": cv_t2, "cv_mt": cv_mt,
                  "ca_exp": ca_exp, "cv_bd": cv_bd, "t_st": ct_st, "t_yd": ct_yd,
                  "ca_y": ca_y, "cv_y": cv_y}
        ssd_prev = pv
        p.dma("gpsimd", y_store[j, :, :], ypair[:, :], waits=[p.c("v")])
        if stop_after.startswith("p1mt") or stop_after in ("p1pair0", "p1st"):
            break
    ca_p1c = p.c("a")
    if stop_after.startswith("p1mt") or stop_after in ("p1pair0", "p1xt", "p1ct", "p1st"):
        for nm, src_, wd in (("states", states, NCH * 1024), ("ypair", ypair, NCH * 128)):
            if nm in taps_out:
                p.dma("gpsimd", taps_out[nm][:, :], src_[:, 0:wd],
                      waits=[(ca_p1c), p.c("v")])
        p.emit("gpsimd", lambda e: e.engine_nop(), waits=p.allc())
        with nc.Block() as block:
            p.build(block)
        return nc, taps_out

    def scan(e):
        e.memset(gst1[:, 0:1024], 0.0)
        last = None
        for c in range(NCH):
            for t in range(NPAIR):
                sl = slice(1024 * c + 64 * t, 1024 * c + 64 * (t + 1))
                rn = slice(64 * t, 64 * (t + 1))
                e.tensor_copy(gst2[:, 0:64], states[:, sl])
                e.tensor_copy(states[:, sl], gst1[:, rn])
                last = e.scalar_tensor_tensor(gst1[:, rn], gst1[:, rn],
                                              dcolp[:, 8 * t + c:8 * t + c + 1],
                                              gst2[:, 0:64], OP.mult, OP.add)
        return last
    cv_scan = p.emit("vector", scan, waits=[ca_p1c, cv_cols, p.c("qg"), p.c("g")],
                     inc="v")
    if stop_after == "p1x":
        for nm, src_, pr, wd in (("states", states, 128, NCH * 1024),
                                 ("sumout", gst1, 128, 1024)):
            if nm in taps_out:
                p.dma("gpsimd", taps_out[nm][:, :], src_[0:pr, 0:wd],
                      waits=[cv_scan])
        p.emit("gpsimd", lambda e: e.engine_nop(), waits=p.allc())
        with nc.Block() as block:
            p.build(block)
        return nc, taps_out
    cv = p.emit("vector", lambda e: e.tensor_reduce(
                sp_st[0:1, 0:32],
                dlogrow[0:1, 0:256].rearrange("o (c h) -> o h c", h=32),
                axis=AX.X, op=OP.add), waits=[cv_scan], inc="v")
    p.dma("gpsimd", cc_in_t.ap()[0:128, 0:1024], gst1[:, 0:1024], waits=[cv_scan])
    p.dma("gpsimd", cc_in_t.ap()[0:128, 1024:1056], sp_st[:, 0:32], waits=[cv])
    if "dlt9m" in taps_out:
        p.dma("gpsimd", taps_out["dlt9m"][:, :], dlt9[:, :], waits=[p.c("v")])
    cg_ag = p.emit("gpsimd", lambda e: e.collective_compute(
                   "AllGather", OP.bypass, replica_groups=[list(range(NCORES))],
                   ins=[cc_in_t.ap().opt()], outs=[cc_out_t.ap().opt()]),
                   waits=[p.c("qg")], inc="g")

    if stop_after == "p1":
        for nm, src, pr, wd in (("states", states, 128, NCH * 1024),
                                ("sumout", gst1, 128, 1024),
                                ("ccout", cc_out, None, None)):
            if nm in taps_out:
                if nm == "ccout":
                    p.dma("gpsimd", taps_out[nm][:, :], cc_out[:, :],
                          waits=[cg_ag])
                else:
                    p.dma("gpsimd", taps_out[nm][:, :], src[0:pr, 0:wd],
                          waits=[cv_scan, cg_ag])
        p.emit("gpsimd", lambda e: e.engine_nop(), waits=p.allc())
        with nc.Block() as block:
            p.build(block)
        return nc, taps_out

    stacks["s1"].close()
    wq = sbuf("wq", (128, NKC * 128), BF16, st="s2")
    wq2 = sbuf("wq2", (128, NKC * 128), BF16, st="s2")
    wqB = sbuf("wqB", (128, NKC * 128), BF16, st="s2")
    wq2B = sbuf("wq2B", (128, NKC * 128), BF16, st="s2")
    hid_hi = sbuf("hid_hi", (128, JBLK * 1024), BF16, st="s2")
    hid_lo = sbuf("hid_lo", (128, JBLK * 1024), BF16, st="s2")

    jbase = 0
    wpairs = [(wq, wq2), (wqB, wq2B)]
    lastmm = {0: None, 1: None}
    prev_sig = None
    prev_x2f = None
    prev_hid = None
    prev_q = None
    for blk, JB in enumerate(JBLOCKS):
        for jj in range(JB):
            j = jbase + jj
            par = jj % 2
            qd = {}
            for wi, wsrc, dst in ((0, w1T, wpairs[par][0]), (1, w2T, wpairs[par][1])):
                src_ = wsrc.rearrange("(k p) e -> p k e", p=128)[:, :, 128 * j:128 * (j + 1)]
                wl = p.dma("sync", wstage[:, :].rearrange("p (k e) -> p k e", e=128),
                           src_, waits=[prev_q] if prev_q else [p.c("v"), p.c("t")])

                def quant(e, dst=dst, wi=wi):
                    e.tensor_scalar(whi[:, :], wstage[:, :], th_col[:, wi:wi + 1],
                                    0.0, OP.is_ge, OP.add)
                    e.tensor_scalar(wlo[:, :], wstage[:, :],
                                    th_col[:, 3 + wi:4 + wi], -1.0,
                                    OP.is_le, OP.mult)
                    return e.tensor_tensor(dst[:, :], whi[:, :], wlo[:, :], OP.add)
                qd[wi] = p.emit("vector", quant,
                                waits=list(wl) + ([lastmm[par]] if lastmm[par]
                                                  else [p.c("t")]), inc="v")
                prev_q = qd[wi]
            for ts in range(2):
                tok0 = HALO + 512 * ts
                for wi, wqt in ((0, wpairs[par][0]), (1, wpairs[par][1])):
                    idx = 0
                    for k in range(NKC):
                        for xt_ in (xn_hi, xn_lo):
                            st, sp = idx == 0, idx == 2 * NKC - 1
                            wv = []
                            if idx == 0:
                                wv = [qd[wi]]
                                pb_free = prev_sig if wi == 0 else prev_x2f
                                if pb_free:
                                    wv.append(pb_free)
                            p.emit("tensor", lambda e, k=k, x_=xt_, w_=wqt, wi=wi,
                                   st=st, sp=sp, tok0=tok0:
                                   e.matmul(PB[wi][:, 0:512],
                                            w_[:, 128 * k:128 * (k + 1)],
                                            x_[:, k * TW + tok0:k * TW + tok0 + 512],
                                            start=st, stop=sp),
                                   waits=wv, inc=("t" if sp else None))
                            idx += 1
                ct = p.c("t")
                lastmm[par] = ("t", ct[1])
                prev_sig = p.emit("scalar", lambda e: e.activation(gst1[:, 0:512],
                                  PB[0][:, 0:512], AF.Sigmoid),
                                  waits=[ct] + ([prev_hid] if prev_hid else []),
                                  inc="a")
                prev_x2f = p.emit("scalar", lambda e: e.activation(gst2[:, 0:512],
                                  PB[1][:, 0:512], AF.Copy), waits=[prev_sig], inc="a")

                def mk_hid(e, jj=jj, ts=ts):
                    off = 1024 * jj + 512 * ts
                    e.tensor_tensor(gst3[:, 0:512], gst1[:, 0:512], gst2[:, 0:512],
                                    OP.mult)
                    e.tensor_copy(hid_hi[:, off:off + 512], gst3[:, 0:512])
                    return e.scalar_tensor_tensor(hid_lo[:, off:off + 512],
                                                  gst3[:, 0:512], 1.0,
                                                  hid_hi[:, off:off + 512],
                                                  OP.mult, OP.subtract)
                prev_hid = p.emit("vector", mk_hid, waits=[prev_x2f], inc="v")
        cv_hid = p.c("v")
        last3 = {0: None, 1: None}
        prev_q3 = None
        for dq in range(4):
            for jj in range(JB):
                j = jbase + jj
                par = jj % 2
                buf3 = wpairs[par][0]
                src_ = w3T[128 * j:128 * (j + 1), 512 * dq:512 * (dq + 1)]
                wl = p.dma("sync", wstage[:, 0:512], src_,
                           waits=[prev_q3] if prev_q3 else [p.c("v"), p.c("t")])

                def quant3(e, buf3=buf3):
                    e.tensor_scalar(whi[:, 0:512], wstage[:, 0:512], th_col[:, 2:3],
                                    0.0, OP.is_ge, OP.add)
                    e.tensor_scalar(wlo[:, 0:512], wstage[:, 0:512], th_col[:, 5:6],
                                    -1.0, OP.is_le, OP.mult)
                    return e.tensor_tensor(buf3[:, 0:512], whi[:, 0:512],
                                           wlo[:, 0:512], OP.add)
                cvq = p.emit("vector", quant3,
                             waits=list(wl) + [cv_hid] +
                                   ([last3[par]] if last3[par] else [p.c("t")]),
                             inc="v")
                prev_q3 = cvq
                ctl = None
                for d4 in range(4):
                    for ts in range(2):
                        for pi, ht in enumerate((hid_hi, hid_lo)):
                            st = (jj == 0 and pi == 0)
                            sp = (jj == JB - 1 and pi == 1)
                            ctl = p.emit("tensor",
                                   lambda e, d4=d4, ts=ts, h_=ht, jj=jj, st=st, sp=sp,
                                   buf3=buf3:
                                   e.matmul(PB[2 * d4 + ts][:, 0:512],
                                            buf3[:, 128 * d4:128 * (d4 + 1)],
                                            h_[:, 1024 * jj + 512 * ts:1024 * jj + 512 * ts + 512],
                                            start=st, stop=sp),
                                   waits=[cvq, p.c("a")] if (d4 == 0 and ts == 0 and pi == 0 and jj == 0)
                                         else ([cvq] if (d4 == 0 and ts == 0 and pi == 0) else ()),
                                   inc=("t" if (d4 == 3 and ts == 1 and pi == 1) else None))
                last3[par] = ctl
            ct = p.c("t")
            for d4 in range(4):
                d = 4 * dq + d4
                if blk < len(JBLOCKS) - 1:
                    if blk == 0:
                        def ev0(e, d4=d4):
                            e.tensor_copy(gst1[:, 0:512], PB[2 * d4][:, 0:512])
                            return e.tensor_copy(gst1[:, 512:1024],
                                                 PB[2 * d4 + 1][:, 0:512])
                        cv = p.emit("vector", ev0, waits=[ct, p.c("qg")], inc="v")
                    else:
                        wl0 = p.dma("sync", gst2[:, 0:1024], fp_store[d, :, :],
                                    waits=[p.c("v"), p.c("qg")])
                        def ev0b(e, d4=d4):
                            e.tensor_tensor(gst1[:, 0:512], PB[2 * d4][:, 0:512],
                                            gst2[:, 0:512], OP.add)
                            return e.tensor_tensor(gst1[:, 512:1024],
                                                   PB[2 * d4 + 1][:, 0:512],
                                                   gst2[:, 512:1024], OP.add)
                        cv = p.emit("vector", ev0b,
                                    waits=[ct, p.c("qg")] + list(wl0), inc="v")
                    p.dma("gpsimd", fp_store[d, :, :], gst1[:, 0:1024], waits=[cv])
                else:
                    wl = p.dma("sync", gst2[:, 0:1024], fp_store[d, :, :],
                               waits=[p.c("v"), p.c("qg")])

                    def ev1(e, d4=d4):
                        e.tensor_tensor(gst1[:, 0:512], PB[2 * d4][:, 0:512],
                                        gst2[:, 0:512], OP.add)
                        return e.tensor_tensor(gst1[:, 512:1024],
                                               PB[2 * d4 + 1][:, 0:512],
                                               gst2[:, 512:1024], OP.add)
                    cv = p.emit("vector", ev1, waits=[ct] + list(wl), inc="v")
                    ca = p.emit("scalar", lambda e: e.activation(gst3[:, 0:1024],
                                gst1[:, 0:1024], AF.Silu),
                                waits=[cv, p.c("qg")], inc="a")
                    p.dma("gpsimd", g_store[d, :, :], gst3[:, 0:1024], waits=[ca])
            prev_q3 = None
            last3 = {0: None, 1: None}
        jbase += JB
        lastmm = {0: None, 1: None}
        prev_q = None

    if stop_after == "p2":
        p.emit("gpsimd", lambda e: e.engine_nop(), waits=p.allc())
        with nc.Block() as block:
            p.build(block)
        return nc, taps_out

    if "gout" in taps_out:
        p.dma("gpsimd", taps_out["gout"][:, :].rearrange("(j p) f -> j p f", p=128),
              g_store[:, :, :], waits=[p.c("qg"), p.c("a"), p.c("v"), p.c("t")])
    if "hid0" in taps_out:
        p.dma("gpsimd", taps_out["hid0"][:, :], hid_hi[:, 0:1024],
              waits=[p.c("qg"), p.c("a"), p.c("v"), p.c("t")])
    stacks["s2"].close()
    sglob = sbuf("sglob", (128, 1024), st="s3")
    sfull = sbuf("sfull", (128, 1024), st="s3")
    ynh = sbuf("ynh", (128, NKC * 512), BF16, st="s3")
    ynl = sbuf("ynl", (128, NKC * 512), BF16, st="s3")
    ytmp3 = sbuf("ytmp3", (128, 128), st="s3")
    rmsst = sbuf("rmsst", (128, 64), st="s3")

    p.emit("vector", lambda e: e.memset(sglob[:, :], 0.0), waits=[p.c("a"), p.c("t")],
           inc="v")
    wl = p.dma("sync", dlt9[0:8, :],
               cc_out.rearrange("(j p) f -> j p f", p=128)[:, 0:1, 1024:1056],
               waits=[cg_ag, p.c("v")])
    ct = p.emit("tensor", lambda e: e.matmul(PB[6][0:32, 0:8], dlt9[:, :], rhs9s[:, :],
                start=True, stop=True), waits=list(wl) + [p.c("a"), p.c("v")], inc="t")
    ca = p.emit("scalar", lambda e: e.activation(coefT[:, :], PB[6][0:32, 0:8], AF.Exp),
                waits=[ct], inc="a")
    for t in range(NPAIR):
        bk = PB[6] if t % 2 == 0 else PB[7]
        ct = p.emit("tensor", lambda e, t=t, bk=bk: e.matmul(bk[:, 0:8],
                    pairsel[:, 128 * t:128 * (t + 1)], coefT[:, :],
                    start=True, stop=True), waits=[ca, p.c("v"), p.c("a")], inc="t")
        p.emit("vector", lambda e, t=t, bk=bk: e.tensor_copy(coefc[:, 8 * t:8 * (t + 1)],
               bk[:, 0:8]), waits=[ct], inc="v")
    for jcore in range(NCORES):
        wl = p.dma("sync", gst1[:, 0:1024],
                   cc_out.rearrange("(j p) f -> j p f", p=128)[jcore, :, 0:1024],
                   waits=[p.c("v")])

        def acc_sg(e, jcore=jcore):
            last = None
            for t in range(NPAIR):
                last = e.scalar_tensor_tensor(sglob[:, 64 * t:64 * (t + 1)],
                                              gst1[:, 64 * t:64 * (t + 1)],
                                              coefc[:, 8 * t + jcore:8 * t + jcore + 1],
                                              sglob[:, 64 * t:64 * (t + 1)],
                                              OP.mult, OP.add)
            return last
        p.emit("vector", acc_sg, waits=list(wl) + [p.c("v")], inc="v")
    cv_sglob = p.c("v")
    if "sg0" in taps_out:
        p.dma("gpsimd", taps_out["sg0"][:, :], sglob[:, :], waits=[cv_sglob])
    if "coefT" in taps_out:
        p.dma("gpsimd", taps_out["coefT"][:, :], coefT[:, :], waits=[cv_sglob])
    if "coefc" in taps_out:
        p.dma("gpsimd", taps_out["coefc"][:, :], coefc[:, :], waits=[cv_sglob])
    if "dlt9" in taps_out:
        p.dma("gpsimd", taps_out["dlt9"][:, :], dlt9[:, :], waits=[cv_sglob])
    if "sjlast" in taps_out:
        p.dma("gpsimd", taps_out["sjlast"][:, :], gst1[:, 0:1024], waits=[cv_sglob])
    wl_rms = p.dma("sync", zb[:, :], rms_h[:, :], waits=[cv_sglob, p.c("a")])

    for half in range(2):
        for ci in range(4):
            c = 4 * half + ci

            def mk_sfull(e, c=c):
                last = None
                for t in range(NPAIR):
                    last = e.scalar_tensor_tensor(
                        sfull[:, 64 * t:64 * (t + 1)], sglob[:, 64 * t:64 * (t + 1)],
                        pdcol[:, 8 * t + c:8 * t + c + 1],
                        states[:, 1024 * c + 64 * t:1024 * c + 64 * (t + 1)],
                        OP.mult, OP.add)
                return last
            cv_sf = p.emit("vector", mk_sfull, waits=[cv_sglob, p.c("a"), p.c("t")],
                           inc="v")
            wl_c = p.dma("sync", gst1[:, 0:2048].rearrange("p (j f) -> p j f", f=128),
                         c_store.rearrange("j p f -> p j f")[:, :, 128 * c:128 * (c + 1)],
                         waits=[p.c("v")])
            wl_y = p.dma("sync", gst2[:, 0:2048].rearrange("p (j f) -> p j f", f=128),
                         y_store.rearrange("j p f -> p j f")[:, :, 128 * c:128 * (c + 1)],
                         waits=[p.c("v")])
            wl_s = p.dma("sync", gst3[:, 0:2048].rearrange("p (j f) -> p j f", f=128),
                         sz_store.rearrange("j p f -> p j f")[:, :, 128 * c:128 * (c + 1)],
                         waits=[p.c("v")])
            for t in range(NPAIR):
                p.emit("tensor", lambda e, t=t: e.matmul(PB[4][:, 0:64],
                       gst1[0:64, 128 * t:128 * (t + 1)],
                       sfull[0:64, 64 * t:64 * (t + 1)], start=True, stop=True),
                       waits=(list(wl_s) + [cv_sf, p.c("a"), p.c("v")]) if t == 0
                             else [p.c("a")])
                ct = p.emit("tensor", lambda e, t=t: e.matmul(PB[5][:, 0:64],
                            gst1[64:128, 128 * t:128 * (t + 1)],
                            sfull[64:128, 64 * t:64 * (t + 1)],
                            start=True, stop=True), inc="t")

                def ev_y(e, t=t, c=c):
                    e.activation(ytmp3[:, 0:64], PB[4][:, 0:64], AF.Copy,
                                 scale=eacs[:, 32 * c + 2 * t:32 * c + 2 * t + 1])
                    return e.activation(ytmp3[:, 64:128], PB[5][:, 0:64], AF.Copy,
                                        scale=eacs[:, 32 * c + 2 * t + 1:32 * c + 2 * t + 2])
                ca = p.emit("scalar", ev_y,
                            waits=[ct] + (list(wl_s) if t == 0 else []),
                            inc="a")

                def mk_ygf(e, t=t):
                    e.tensor_tensor(ytmp3[:, :], ytmp3[:, :],
                                    gst2[:, 128 * t:128 * (t + 1)], OP.add)
                    return e.tensor_tensor(gst2[:, 128 * t:128 * (t + 1)], ytmp3[:, :],
                                           gst3[:, 128 * t:128 * (t + 1)], OP.mult)
                p.emit("vector", mk_ygf, waits=[ca], inc="v")
            cv_ygf = p.c("v")
            if c == 0 and "ygf0" in taps_out:
                p.dma("gpsimd", taps_out["ygf0"][:, :], gst2[:, 0:2048],
                      waits=[p.c("v")])
            ca = p.emit("scalar", lambda e: e.activation(gst1[:, 0:2048],
                        gst2[:, 0:2048], AF.Square), waits=[cv_ygf], inc="a")

            def mk_rms(e):
                i1 = e.tensor_reduce(rmsst[:, 0:1], gst1[:, 0:2048], axis=AX.X,
                                     op=OP.add)
                p.chain(e, i1)
                i2 = e.tensor_scalar(rmsst[:, 1:2], rmsst[:, 0:1], 1.0 / DM, 1e-5,
                                     OP.mult, OP.add)
                p.chain(e, i2)
                return e.reciprocal(rmsst[:, 2:3], rmsst[:, 1:2])
            cv = p.emit("vector", mk_rms, waits=[ca], inc="v")
            ca = p.emit("scalar", lambda e: e.activation(rmsst[:, 3:4], rmsst[:, 2:3],
                        AF.Sqrt), waits=[cv], inc="a")

            def rms_fin(e):
                y = rmsst[:, 3:4]
                i1 = e.tensor_tensor(rmsst[:, 4:5], y, y, OP.mult)
                p.chain(e, i1)
                i2 = e.tensor_tensor(rmsst[:, 4:5], rmsst[:, 4:5], rmsst[:, 1:2],
                                     OP.mult)
                p.chain(e, i2)
                i3 = e.tensor_scalar(rmsst[:, 4:5], rmsst[:, 4:5], -0.5, 1.5,
                                     OP.mult, OP.add)
                p.chain(e, i3)
                i4 = e.tensor_tensor(y, y, rmsst[:, 4:5], OP.mult)
                p.chain(e, i4)
                e.tensor_scalar(gst2[:, 0:2048], gst2[:, 0:2048], y, 0.0,
                                OP.mult, OP.add)
                return e.tensor_tensor(gst2[:, 0:2048], gst2[:, 0:2048], zb[:, :],
                                       OP.mult)
            cv_yn = p.emit("vector", rms_fin, waits=[ca] + list(wl_s), inc="v")
            if c == 0 and "yn0" in taps_out:
                p.dma("gpsimd", taps_out["yn0"][:, :], gst2[:, 0:2048],
                      waits=[cv_yn])
            if c == 0 and "rmsst" in taps_out:
                p.dma("gpsimd", taps_out["rmsst"][:, :], rmsst[:, 0:8],
                      waits=[cv_yn])
            if c == 0 and "sq0" in taps_out:
                p.dma("gpsimd", taps_out["sq0"][:, :], gst1[:, 0:2048],
                      waits=[cv_yn])
            if stop_after == "p3c0":
                for nm, srcb, wd in (("rmsstp", rmsst, 8), ("yn0p", gst2, 2048),
                                     ("sq0p", gst1, 2048), ("ygf0p", None, 0)):
                    if nm in taps_out and srcb is not None:
                        p.dma("gpsimd", taps_out[nm][:, :], srcb[:, 0:wd],
                              waits=[cv_yn])
                p.emit("gpsimd", lambda e: e.engine_nop(), waits=p.allc())
                with nc.Block() as block:
                    p.build(block)
                return nc, taps_out
            for q4 in range(4):
                for ti in range(4):
                    e_t = 4 * q4 + ti
                    p.emit("tensor", lambda e, e_t=e_t, ti=ti: e.transpose(
                           PB[5][:, 128 * ti:128 * (ti + 1)],
                           gst2[:, 128 * e_t:128 * (e_t + 1)], identc[:, :]),
                           waits=[cv_yn, p.c("v"), p.c("a")] if ti == 0 else (),
                           inc=("t" if ti == 3 else None))
                ct = p.c("t")

                def ev_yn(e, q4=q4, ci=ci):
                    last = None
                    for ti in range(4):
                        e_t = 4 * q4 + ti
                        dsth = ynh[:, 512 * e_t + 128 * ci:512 * e_t + 128 * (ci + 1)]
                        dstl = ynl[:, 512 * e_t + 128 * ci:512 * e_t + 128 * (ci + 1)]
                        e.tensor_copy(dsth, PB[5][:, 128 * ti:128 * (ti + 1)])
                        last = e.scalar_tensor_tensor(
                            dstl, PB[5][:, 128 * ti:128 * (ti + 1)], 1.0, dsth,
                            OP.mult, OP.subtract)
                    return last
                p.emit("vector", ev_yn, waits=[ct], inc="v")
        cv_yn_half = p.c("v")
        for dq in range(2):
            for et in range(NKC):
                src = w_outT.rearrange("(k p) d -> p k d", p=128)[
                    :, et:et + 1, 1024 * dq:1024 * (dq + 1)]
                w = list(lw_state["spl"]) if lw_state["spl"] else [p.c("t")]
                wl = p.dma("sync", wstage[:, 0:1024].rearrange("p (k d) -> p k d", d=1024),
                           src, waits=w)
                ca = p.emit("scalar", lambda e: e.activation(whi[:, 0:1024],
                            wstage[:, 0:1024], AF.Copy),
                            waits=list(wl) + [p.c("t")], inc="a")
                cv = p.emit("vector", lambda e: e.scalar_tensor_tensor(wlo[:, 0:1024],
                            wstage[:, 0:1024], 1.0, whi[:, 0:1024], OP.mult,
                            OP.subtract), waits=[ca], inc="v")
                lw_state["spl"] = (ca, cv)
                for d8 in range(8):
                    for (wt_, yt_) in ((whi, ynh), (whi, ynl), (wlo, ynh)):
                        st = (et == 0 and yt_ is ynh and wt_ is whi)
                        sp = (et == NKC - 1 and wt_ is wlo)
                        p.emit("tensor", lambda e, d8=d8, w_=wt_, y_=yt_, et=et,
                               st=st, sp=sp:
                               e.matmul(PB[d8][:, 0:512],
                                        w_[:, 128 * d8:128 * (d8 + 1)],
                                        y_[:, 512 * et:512 * (et + 1)],
                                        start=st, stop=sp),
                               waits=[cv, cv_yn_half, p.c("a")] if (d8 == 0 and wt_ is whi and yt_ is ynh and et == 0) else ([cv] if (d8 == 0 and wt_ is whi and yt_ is ynh) else ()),
                               inc=("t" if (d8 == 7 and wt_ is wlo) else None))
            ct = p.c("t")
            for d8 in range(8):
                d = 8 * dq + d8
                wl_g = p.dma("sync", gst1[:, 0:512],
                             g_store[d, :, 512 * half:512 * (half + 1)],
                             waits=[p.c("v")])
                wl_u = p.dma("sync", gst1[:, 512:1024],
                             u_fm[128 * d:128 * (d + 1),
                                  HALO + 512 * half:HALO + 512 * (half + 1)],
                             waits=[p.c("v")])

                def ev_out(e, d8=d8):
                    e.tensor_tensor(gst2[:, 0:512], PB[d8][:, 0:512], gst1[:, 0:512],
                                    OP.mult)
                    return e.tensor_tensor(gst2[:, 0:512], gst2[:, 0:512],
                                           gst1[:, 512:1024], OP.add)
                cv = p.emit("vector", ev_out,
                            waits=[ct, p.c("qg")] + list(wl_g) + list(wl_u), inc="v")
                p.dma("gpsimd", out_fm[128 * d:128 * (d + 1),
                                       512 * half:512 * (half + 1)],
                      gst2[:, 0:512], waits=[cv])

    p.emit("gpsimd", lambda e: e.engine_nop(), waits=p.allc())
    with nc.Block() as block:
        p.build(block)
    return nc, taps_out


def _host_prep(inputs):
    u = np.ascontiguousarray(np.asarray(inputs["L0_attn_u"], np.float32)[0])
    uT = np.ascontiguousarray(u.T)
    w_inT = np.ascontiguousarray(np.asarray(inputs["L0_attn_W_in"], np.float32).T)
    w_outT = np.ascontiguousarray(np.asarray(inputs["L0_attn_W_out"], np.float32).T)
    w1T = np.ascontiguousarray(np.asarray(inputs["L0_ffn_w1"], np.float32).T)
    w2T = np.ascontiguousarray(np.asarray(inputs["L0_ffn_w2"], np.float32).T)
    w3T = np.ascontiguousarray(np.asarray(inputs["L0_ffn_w3"], np.float32).T)
    conv_w = np.ascontiguousarray(np.asarray(inputs["L0_attn_conv_w"], np.float32))
    conv_b = np.ascontiguousarray(np.asarray(inputs["L0_attn_conv_b"], np.float32))[:, None]
    zb = np.ascontiguousarray(np.broadcast_to(
        np.asarray(inputs["L0_attn_z_bias"], np.float32), (128, DM)))
    rms = np.ascontiguousarray(np.broadcast_to(
        np.asarray(inputs["L0_attn_rms_w"], np.float32), (128, DM)))
    drep = np.ascontiguousarray(np.broadcast_to(
        np.asarray(inputs["L0_attn_D"], np.float32), (128, NH)))
    consts = host_constants()
    in_maps = []
    for i in range(NCORES):
        t0 = i * TOK
        ufm = np.zeros((DM, TW), np.float32)
        lo = max(0, t0 - HALO)
        ufm[:, HALO - (t0 - lo):] = uT[:, lo:t0 + TOK]
        m = dict(u_fm=ufm, w_inT=w_inT, w_outT=w_outT, w1T=w1T, w2T=w2T, w3T=w3T,
                 conv_w=conv_w, conv_b=conv_b, z_bias_b=zb, rms_b=rms, d_rep=drep,
                 rhs9=per_core_rhs9(i), **consts)
        in_maps.append(m)
    return in_maps


def kernel(**inputs):
    in_maps = _host_prep(inputs)
    nc, _ = build_program("full")
    res = run_bass_kernel_spmd(nc, in_maps, core_ids=list(range(NCORES)))
    out = np.concatenate([r["out_fm"] for r in res.results], axis=1)
    return np.ascontiguousarray(out.T)[None].astype(np.float32)


# version 10
# speedup vs baseline: 1.0256x; 1.0256x over previous
import contextlib
import numpy as np
import concourse.bass as bass
import concourse.mybir as mybir
from concourse.bass_utils import run_bass_kernel_spmd

F32 = mybir.dt.float32
BF16 = mybir.dt.bfloat16
AF = mybir.ActivationFunctionType
OP = mybir.AluOpType
AX = mybir.AxisListType

NCORES = 8
TOK = 1024
HALO = 8
TW = TOK + HALO
NCH = 8
DM = 2048
NH = 32
NPAIR = 16
DH = 5632
NKC = DM // 128
NJT = DH // 128
NEG = -1e30
EPSQ = 1e-8
NS = (344, 344, 344)
NSOFF = (0, 344, 688)
JBLK = 7
JBLOCKS = (7, 7, 7, 7, 7, 7, 2)


class Prog:
    def __init__(self, nc):
        self.nc = nc
        self.ops = {k: [] for k in ("sync", "tensor", "vector", "scalar", "gpsimd")}
        self.sems = {}
        self.counts = {}
        for name in ("qs", "qg", "qc", "b0", "b1", "t", "v", "a", "g", "x"):
            h = nc._stack.enter_context(nc.semaphore("sem_" + name))
            self.sems[name] = h
            self.counts[name] = 0

    def emit(self, eng, fn, waits=(), inc=None, inc_n=1):
        if inc is not None:
            self.counts[inc] += inc_n
        self.ops[eng].append((fn, tuple(waits), inc, inc_n))
        return (inc, self.counts[inc]) if inc else None

    def dma(self, queue, out, in_, waits=(), sem=None):
        inc = sem or {"sync": "qs", "gpsimd": "qg", "scalar": "qc"}[queue]
        self.emit(queue, lambda e, o=out, i=in_: e.dma_start(out=o, in_=i),
                  waits=waits, inc=inc, inc_n=16)
        return ((inc, self.counts[inc]),)

    def c(self, name):
        return (name, self.counts[name])

    def chain(self, e, inst):
        self.counts["x"] += 1
        inst.then_inc(self.sems["x"], 1)
        e.wait_ge(self.sems["x"], self.counts["x"])

    def allc(self):
        return tuple((k, v) for k, v in self.counts.items() if v > 0)

    def build(self, block):
        for eng_name in self.ops:
            def make(eng_name):
                def body(eng):
                    for fn, waits, inc, inc_n in self.ops[eng_name]:
                        for s, v in waits:
                            eng.wait_ge(self.sems[s], v)
                        inst = fn(eng)
                        if inc is not None:
                            inst.then_inc(self.sems[inc], inc_n)
                return body
            getattr(block, eng_name)(make(eng_name))


def host_constants():
    c = {}
    c["identc"] = np.eye(128, dtype=np.float32)
    s = np.arange(128)
    c["neg_triu"] = -np.triu(np.ones((128, 128), np.float32))
    m = np.where(s[None, :] >= s[:, None], 0.0, NEG).astype(np.float32)
    c["maskneg2"] = np.concatenate([m, m], axis=1)
    ps = np.zeros((32, NPAIR * 128), np.float32)
    for t in range(NPAIR):
        ps[2 * t, 128 * t:128 * t + 64] = 1.0
        ps[2 * t + 1, 128 * t + 64:128 * t + 128] = 1.0
    c["pairsel"] = ps
    c["negones"] = -np.ones((128, 1), np.float32)
    o9 = np.zeros((9, 32), np.float32)
    o9[8, :] = 1.0
    c["ones9"] = o9
    sel = np.zeros((2, 256), np.float32)
    sel[0, 0:128] = 1.0
    sel[1, 128:256] = 1.0
    c["rowsel"] = sel
    return c


def per_core_rhs9(i):
    k = np.arange(NCORES)
    maskrow = (k < i).astype(np.float32)
    inc = (k[:, None] <= k[None, :]).astype(np.float32)
    rhs9 = np.zeros((9, NCORES), np.float32)
    rhs9[0:8, :] = maskrow[:, None] - inc
    rhs9[8, :] = np.where(k >= i, NEG, 0.0)
    return rhs9


def build_program(stop_after="full", taps=()):
    nc = bass.Bass("TRN2", target_bir_lowering=False, debug=False,
                   detect_race_conditions=False)
    nc._stack = contextlib.ExitStack()
    ex = nc._stack

    def dram_in(name, shape, dt=F32):
        return nc.dram_tensor(name, list(shape), dt, kind="ExternalInput").ap()

    u_fm = dram_in("u_fm", (DM, TW))
    w_inT = dram_in("w_inT", (DM, 8224))
    w_outT = dram_in("w_outT", (DM, DM))
    w1T = dram_in("w1T", (DM, DH))
    w2T = dram_in("w2T", (DM, DH))
    w3T = dram_in("w3T", (DH, DM))
    conv_w_h = dram_in("conv_w", (6144, 4))
    conv_b_h = dram_in("conv_b", (6144, 1))
    zb_h = dram_in("z_bias_b", (128, DM))
    rms_h = dram_in("rms_b", (128, DM))
    drep_h = dram_in("d_rep", (128, NH))
    ident_h = dram_in("identc", (128, 128))
    negtr_h = dram_in("neg_triu", (128, 128))
    mneg_h = dram_in("maskneg2", (128, 256))
    psel_h = dram_in("pairsel", (32, NPAIR * 128))
    rhs9_h = dram_in("rhs9", (9, NCORES))
    nones_h = dram_in("negones", (128, 1))
    ones9_h = dram_in("ones9", (9, 32))
    rsel_h = dram_in("rowsel", (2, 256))

    out_fm = nc.dram_tensor("out_fm", [DM, TOK], F32, kind="ExternalOutput").ap()

    def dram_scr(name, shape, dt=F32):
        return nc.dram_tensor(name, list(shape), dt).ap()
    c_store = dram_scr("c_store", (NPAIR, 128, TOK))
    y_store = dram_scr("y_store", (NPAIR, 128, NCH * 128))
    sz_store = dram_scr("sz_store", (NKC, 128, NCH * 128))
    g_store = dram_scr("g_store", (NKC, 128, TOK))
    fp_store = dram_scr("fp_store", (NKC, 128, TOK))
    cc_in_t = nc.dram_tensor("cc_in", [128, 1056], F32)
    cc_out_t = nc.dram_tensor("cc_out", [NCORES * 128, 1056], F32)
    cc_out = cc_out_t.ap()

    taps_out = {}
    for tname, tshape in taps:
        taps_out[tname] = nc.dram_tensor("tap_" + tname, list(tshape), F32,
                                         kind="ExternalOutput").ap()

    p = Prog(nc)

    stacks = {k: contextlib.ExitStack() for k in ("s0", "s1", "s2", "s3")}

    def sbuf(name, shape, dt=F32, st=None):
        stk = ex if st is None else stacks[st]
        return stk.enter_context(nc.sbuf_tensor(name, list(shape), dt))

    xn_hi = sbuf("xn_hi", (128, NKC * TW), BF16)
    xn_lo = sbuf("xn_lo", (128, NKC * TW), BF16)
    states = sbuf("states", (128, NCH * 1024))
    acs = sbuf("acs", (128, NCH * 32))
    eacs = sbuf("eacs", (128, NCH * 32))
    ds_tm = sbuf("ds_tm", (128, NCH * 32))
    dlogrow = sbuf("dlogrow", (1, NCH * 32))
    dlogT = sbuf("dlogT", (32, NCH))
    prefT = sbuf("prefT", (32, NCH))
    dcolp = sbuf("dcolp", (128, NPAIR * NCH))
    pdcol = sbuf("pdcol", (128, NPAIR * NCH))
    coefc = sbuf("coefc", (128, NPAIR * NCORES))
    dlt9 = sbuf("dlt9", (9, 32))
    coefT = sbuf("coefT", (32, NCORES))
    sp_st = sbuf("sp_st", (128, 32))
    identc = sbuf("identc_s", (128, 128))
    neg_triu = sbuf("neg_triu_s", (128, 128))
    maskneg2 = sbuf("maskneg2_s", (128, 256))
    pairsel = sbuf("pairsel_s", (32, NPAIR * 128))
    rhs9s = sbuf("rhs9s", (9, NCORES))
    negones = sbuf("negones_s", (128, 1))
    rowsel = sbuf("rowsel_s", (2, 256))
    ones_row = sbuf("ones_row", (1, 128))
    ones_col = sbuf("ones_col", (128, 1))
    ones_col_b = sbuf("ones_col_b", (128, 1), BF16)
    d_rep = sbuf("d_rep_s", (128, NH))
    zb = sbuf("zb", (128, DM))
    conv_ws = sbuf("conv_ws", (128, 48 * 4))
    conv_bs = sbuf("conv_bs", (128, 48))
    th_col = sbuf("th_col", (128, 8))
    gam = sbuf("gam", (1, 16))
    wstage = sbuf("wstage", (128, NKC * 128))
    whi = sbuf("whi", (128, NKC * 128), BF16)
    wlo = sbuf("wlo", (128, NKC * 128), BF16)
    gst1 = sbuf("gst1", (128, 2048))
    gst2 = sbuf("gst2", (128, 2048))
    gst3 = sbuf("gst3", (128, 2048))

    PB = [ex.enter_context(nc.psum_tensor(f"pb{i}", [128, 512], F32)) for i in range(8)]

    p.dma("sync", identc[:, :], ident_h[:, :])
    p.dma("sync", neg_triu[:, :], negtr_h[:, :])
    p.dma("sync", maskneg2[:, :], mneg_h[:, :])
    p.dma("sync", pairsel[:, :], psel_h[:, :])
    p.dma("sync", rhs9s[:, :], rhs9_h[:, :])
    p.dma("sync", negones[:, :], nones_h[:, :])
    p.dma("sync", dlt9[:, :], ones9_h[:, :])
    p.dma("sync", rowsel[:, :], rsel_h[:, :])
    p.dma("sync", d_rep[:, :], drep_h[:, :])
    p.dma("sync", zb[:, :], zb_h[:, :])
    p.dma("sync", conv_ws[:, :].rearrange("p (j w) -> p j w", w=4),
          conv_w_h.rearrange("(j p) w -> p j w", p=128))
    p.ops["sync"].append((lambda e: e.dma_start(
        out=conv_bs[:, :].rearrange("p (j w) -> p j w", w=1),
        in_=conv_b_h.rearrange("(j p) w -> p j w", p=128),
        allow_slow_non_contiguous=True), (), "qs", 16))
    p.counts["qs"] += 16
    wconsts = ((("qs"), p.counts["qs"]),)

    def init_v(e):
        e.memset(ones_row[:, :], 1.0)
        e.memset(ones_col[:, :], 1.0)
        e.memset(ones_col_b[:, :], 1.0)
        e.memset(states[:, :], 0.0)
        return e.memset(prefT[:, 0:1], 0.0)
    p.emit("vector", init_v, waits=wconsts, inc="v")

    ustage = sbuf("ustage", (128, 2 * TW), st="s0")
    stat = sbuf("stat", (1, 2 * TW), st="s0")

    if "dlt9e" in taps_out:
        p.dma("gpsimd", taps_out["dlt9e"][:, :], dlt9[:, :], waits=[p.c("qs")])

    for k in range(NKC):
        half = k % 2
        buf = ustage[:, half * TW:half * TW + TW]
        w = [p.c("t")] if k >= 2 else [p.c("v")]
        wld = p.dma("sync", buf, u_fm[128 * k:128 * k + 128, :], waits=w,
                    sem=f"b{half}")
        ca = p.emit("scalar", lambda e, b=buf: e.activation(gst1[:, 0:TW], b, AF.Square),
                    waits=list(wld) + [p.c("t")], inc="a")
        for si in range(3):
            off, n = NSOFF[si], NS[si]
            p.emit("tensor", lambda e, b=buf, o=off, n=n, k=k, si=si:
                   e.matmul(PB[si][0:1, 0:n], ones_col[:, :], b[:, o:o + n],
                            start=(k == 0), stop=(k == NKC - 1)),
                   waits=list(wld))
        for si in range(3):
            off, n = NSOFF[si], NS[si]
            p.emit("tensor", lambda e, o=off, n=n, k=k, si=si:
                   e.matmul(PB[3 + si][0:1, 0:n], ones_col[:, :], gst1[:, o:o + n],
                            start=(k == 0), stop=(k == NKC - 1)),
                   waits=[ca], inc=("t" if si == 2 else None))
    ct_stats = p.c("t")

    def fin_stats(e):
        for si in range(3):
            off, n = NSOFF[si], NS[si]
            e.tensor_scalar(stat[0:1, off:off + n], PB[si][0:1, 0:n], 1.0 / DM, 0.0,
                            OP.mult, OP.add)
            e.tensor_scalar(stat[0:1, TW + off:TW + off + n], PB[3 + si][0:1, 0:n],
                            1.0 / DM, 0.0, OP.mult, OP.add)
        m = stat[0:1, 0:TW]
        var = stat[0:1, TW:2 * TW]
        msq = ustage[0:1, 0:TW]
        i1 = e.tensor_tensor(msq, m, m, OP.mult)
        p.chain(e, i1)
        i2 = e.tensor_tensor(var, var, msq, OP.subtract)
        p.chain(e, i2)
        i3 = e.tensor_scalar(var, var, 1e-5, 0.0, OP.add, OP.add)
        p.chain(e, i3)
        return e.reciprocal(ustage[0:1, TW:2 * TW], var)
    cv = p.emit("vector", fin_stats, waits=[ct_stats], inc="v")
    ca = p.emit("scalar", lambda e: e.activation(ustage[0:1, 0:TW],
                ustage[0:1, TW:2 * TW], AF.Sqrt), waits=[cv], inc="a")

    def newton(e):
        y = ustage[0:1, 0:TW]
        var = stat[0:1, TW:2 * TW]
        t = gst1[0:1, 0:TW]
        i1 = e.tensor_tensor(t, y, y, OP.mult)
        p.chain(e, i1)
        i2 = e.tensor_tensor(t, t, var, OP.mult)
        p.chain(e, i2)
        i3 = e.tensor_scalar(t, t, -0.5, 1.5, OP.mult, OP.add)
        p.chain(e, i3)
        return e.tensor_tensor(y, y, t, OP.mult)
    cv = p.emit("vector", newton, waits=[ca], inc="v")
    for si in range(3):
        off, n = NSOFF[si], NS[si]
        p.emit("tensor", lambda e, o=off, n=n, si=si:
               e.matmul(PB[3 + si][:, 0:n], ones_row[0:1, 0:128], stat[0:1, o:o + n],
                        start=True, stop=True), waits=[cv])
        p.emit("tensor", lambda e, o=off, n=n, si=si:
               e.matmul(PB[si][:, 0:n], ones_row[0:1, 0:128], ustage[0:1, o:o + n],
                        start=True, stop=True),
               waits=[cv], inc=("t" if si == 2 else None))
    ct_bc = p.c("t")

    cv_xn = None
    for k in range(NKC):
        half = k % 2
        buf = ustage[:, half * TW:half * TW + TW]
        w = [p.c("v")] if k >= 2 else [ct_bc]
        wld = p.dma("sync", buf, u_fm[128 * k:128 * k + 128, :], waits=w,
                    sem=f"b{half}")

        def mk_xn(e, b=buf, k=k):
            t = gst1[:, 0:TW]
            for si in range(3):
                off, n = NSOFF[si], NS[si]
                e.tensor_tensor(t[:, off:off + n], b[:, off:off + n],
                                PB[3 + si][:, 0:n], OP.subtract)
                e.tensor_tensor(t[:, off:off + n], t[:, off:off + n],
                                PB[si][:, 0:n], OP.mult)
            e.tensor_copy(xn_hi[:, k * TW:(k + 1) * TW], t)
            return e.scalar_tensor_tensor(xn_lo[:, k * TW:(k + 1) * TW], t, 1.0,
                                          xn_hi[:, k * TW:(k + 1) * TW],
                                          OP.mult, OP.subtract)
        cv_xn = p.emit("vector", mk_xn, waits=list(wld) + [ct_bc], inc="v")

    if "xn" in taps_out:
        for k in range(NKC):
            cvt = p.emit("vector", lambda e, k=k: e.tensor_tensor(gst2[:, 0:TW],
                         xn_hi[:, k * TW:(k + 1) * TW], xn_lo[:, k * TW:(k + 1) * TW],
                         OP.add), waits=[cv_xn, p.c("qg")], inc="v")
            p.dma("gpsimd", taps_out["xn"][128 * k:128 * k + 128, :], gst2[:, 0:TW],
                  waits=[cvt])

    gam_srcs = [(w1T, DM, DH), (w2T, DM, DH), (w3T, DH, DM)]
    prev_abs = {0: None, 1: None}
    prev_mm = {0: None, 1: None}
    for wi, (wt, R, Cw) in enumerate(gam_srcs):
        nrow = R // 128
        cblocks = [(c0, min(1024, Cw - c0)) for c0 in range(0, Cw, 1024)]
        nblk = nrow * len(cblocks)
        bi = 0
        for r in range(nrow):
            for (c0, cw) in cblocks:
                half = bi % 2
                buf = ustage[:, half * TW:half * TW + cw]
                w = ([prev_abs[half]] if prev_abs[half] else [cv_xn])
                wld = p.dma("sync", buf, wt[128 * r:128 * r + 128, c0:c0 + cw],
                            waits=w, sem=f"b{half}")
                wq_ = [prev_mm[half]] if prev_mm[half] else []
                prev_abs[half] = p.emit("scalar", lambda e, b=buf, h=half, cw=cw:
                                        e.activation(whi[:, h * 1024:h * 1024 + cw],
                                                     b, AF.Abs),
                                        waits=list(wld) + wq_, inc="a")
                last = (bi == nblk - 1)
                for s0 in range(0, cw, 512):
                    prev_mm[half] = p.emit("tensor", lambda e, h=half, s0=s0,
                           st=(bi == 0 and s0 == 0 and wi == 0 if False else (bi == 0 and s0 == 0)),
                           sp=(last and s0 + 512 >= cw):
                           e.matmul(PB[6][0:1, 0:512], ones_col_b[:, :],
                                    whi[:, h * 1024 + s0:h * 1024 + s0 + 512],
                                    start=st, stop=sp),
                           waits=[prev_abs[half]], inc="t")
                bi += 1
        ctw = p.c("t")
        p.emit("vector", lambda e, wi=wi: e.tensor_reduce(gam[0:1, wi:wi + 1],
               PB[6][0:1, 0:512], axis=AX.X, op=OP.add), waits=[ctw], inc="v")

    def mk_th(e):
        i1 = e.tensor_scalar(gam[0:1, 4:7], gam[0:1, 0:3], 1.0 / (DH * DM), EPSQ,
                             OP.mult, OP.add)
        p.chain(e, i1)
        e.tensor_scalar(gam[0:1, 8:11], gam[0:1, 4:7], 0.5, 0.0, OP.mult, OP.add)
        return e.tensor_scalar(gam[0:1, 11:14], gam[0:1, 4:7], -0.5, 0.0,
                               OP.mult, OP.add)
    cv = p.emit("vector", mk_th, waits=[p.c("v")], inc="v")
    ct = p.emit("tensor", lambda e: e.matmul(PB[6][0:128, 0:6], ones_row[0:1, 0:128],
                gam[0:1, 8:14], start=True, stop=True), waits=[cv], inc="t")
    cth = p.emit("vector", lambda e: e.tensor_copy(th_col[:, 0:6], PB[6][0:128, 0:6]),
                 waits=[ct], inc="v")
    if "gam" in taps_out:
        p.dma("gpsimd", taps_out["gam"][0:1, :], gam[0:1, 0:16], waits=[cth])

    if stop_after == "p0":
        if "dlt9e" in taps_out:
            p.dma("gpsimd", taps_out["dlt9e"][:, :], dlt9[:, :], waits=[p.c("qs")])
        p.emit("gpsimd", lambda e: e.engine_nop(), waits=p.allc())
        with nc.Block() as block:
            p.build(block)
        return nc, taps_out

    stacks["s0"].close()
    xbc1 = sbuf("xbc1", (128, TW), st="s1")
    xbc2 = sbuf("xbc2", (128, TW), st="s1")
    bsil = sbuf("bsil", (128, TW), st="s1")
    csil = sbuf("csil", (128, TW), st="s1")
    x_tm = sbuf("x_tm", (128, NCH * 128), st="s1")
    bd_tmS = sbuf("bd_tmS", (128, 128), st="s1")
    arow = sbuf("arow", (2, 128), st="s1")
    t2s = sbuf("t2s", (128, 256), st="s1")
    es = sbuf("es", (128, 256), st="s1")
    mts = sbuf("mts", (128, 256), st="s1")
    ytmp = sbuf("ytmp", (128, 128), st="s1")
    ypair = sbuf("ypair", (128, NCH * 128), st="s1")
    szst = sbuf("szst", (128, NCH * 128), st="s1")

    lw_state = {"spl": None}

    def load_win_tile(ecol0, ncols):
        src = w_inT.rearrange("(k p) e -> p k e", p=128)[:, :, ecol0:ecol0 + ncols]
        w = list(lw_state["spl"]) if lw_state["spl"] else [p.c("t"), p.c("a"), p.c("v")]
        wl = p.dma("sync", wstage[:, 0:NKC * ncols].rearrange("p (k e) -> p k e", e=ncols),
                   src, waits=w)
        ca = p.emit("scalar", lambda e, n=ncols: e.activation(whi[:, 0:NKC * n],
                    wstage[:, 0:NKC * n], AF.Copy),
                    waits=list(wl) + [p.c("t")], inc="a")
        cv = p.emit("vector", lambda e, n=ncols: e.scalar_tensor_tensor(
                    wlo[:, 0:NKC * n], wstage[:, 0:NKC * n], 1.0, whi[:, 0:NKC * n],
                    OP.mult, OP.subtract), waits=[ca], inc="v")
        lw_state["spl"] = (ca, cv)
        return cv

    def proj_fm(cw_v, extra=()):
        first = True
        for si in range(3):
            off, n = NSOFF[si], NS[si]
            idx = 0
            for k in range(NKC):
                for (wt_, xt_) in ((whi, xn_hi), (whi, xn_lo), (wlo, xn_hi)):
                    st, sp = idx == 0, idx == 3 * NKC - 1
                    p.emit("tensor", lambda e, k=k, w_=wt_, x_=xt_, o=off, n=n,
                           st=st, sp=sp, si=si:
                           e.matmul(PB[si][:, 0:n], w_[:, 128 * k:128 * (k + 1)],
                                    x_[:, k * TW + o:k * TW + o + n],
                                    start=st, stop=sp),
                           waits=([cw_v] + [w for w in extra if w]) if first else (),
                           inc=("t" if sp and si == 2 else None))
                    first = False
                    idx += 1
        return p.c("t")

    def evac_proj(dst, ct, extra=()):
        def f(e):
            last = None
            for si in range(3):
                off, n = NSOFF[si], NS[si]
                last = e.activation(dst[:, off:off + n], PB[si][:, 0:n], AF.Copy)
            return last
        return p.emit("scalar", f, waits=[ct] + [w for w in extra if w], inc="a")

    def conv_tile(jt, src, dst, waits):
        def f(e):
            w0 = conv_ws[:, 4 * jt + 0:4 * jt + 1]
            b0 = conv_bs[:, jt:jt + 1]
            e.tensor_scalar(dst[:, 0:TOK], src[:, 5:5 + TOK], w0, b0, OP.mult, OP.add)
            last = None
            for w in range(1, 4):
                wc = conv_ws[:, 4 * jt + w:4 * jt + w + 1]
                last = e.scalar_tensor_tensor(dst[:, 0:TOK], src[:, 5 + w:5 + w + TOK],
                                              wc, dst[:, 0:TOK], OP.mult, OP.add)
            return last
        return p.emit("vector", f, waits=waits, inc="v")

    cwv = load_win_tile(8192, 32)
    for c in range(NCH):
        idx = 0
        for k in range(NKC):
            for (wt_, xt_) in ((whi, xn_hi), (whi, xn_lo), (wlo, xn_hi)):
                st, sp = idx == 0, idx == 3 * NKC - 1
                p.emit("tensor", lambda e, k=k, w_=wt_, x_=xt_, c=c, st=st, sp=sp:
                       e.matmul(PB[7][:, 0:32],
                                x_[:, k * TW + HALO + 128 * c:k * TW + HALO + 128 * (c + 1)],
                                w_[:, 32 * k:32 * (k + 1)], start=st, stop=sp),
                       waits=[cwv, p.c("a"), p.c("v")] if idx == 0 else (),
                       inc=("t" if sp else None))
                idx += 1
        ct = p.c("t")
        ca = p.emit("scalar", lambda e: e.activation(gst1[:, 0:32], PB[7][:, 0:32],
                    AF.Exp), waits=[ct], inc="a")
        ca = p.emit("scalar", lambda e: e.activation(sp_st[:, 0:32], gst1[:, 0:32],
                    AF.Ln, bias=1.0), waits=[ca], inc="a")
        ct = p.emit("tensor", lambda e: e.matmul(PB[7][:, 32:64], neg_triu[:, :],
                    sp_st[:, 0:32], start=True, stop=True), waits=[ca], inc="t")
        ca = p.emit("scalar", lambda e, c=c: e.activation(acs[:, 32 * c:32 * (c + 1)],
                    PB[7][:, 32:64], AF.Copy), waits=[ct], inc="a")
        ca = p.emit("scalar", lambda e, c=c: e.activation(eacs[:, 32 * c:32 * (c + 1)],
                    PB[7][:, 32:64], AF.Exp), waits=[ca], inc="a")
        ct = p.emit("tensor", lambda e, c=c: e.matmul(PB[6][0:32, 8 + c:9 + c],
                    sp_st[:, 0:32], negones[:, :], start=True, stop=True),
                    waits=[ca], inc="t")
        cv = p.emit("vector", lambda e, c=c: e.tensor_copy(dlogT[:, c:c + 1],
                    PB[6][0:32, 8 + c:9 + c]), waits=[ct], inc="v")
        ct = p.emit("tensor", lambda e, c=c: e.transpose(PB[6][0:1, 40:72],
                    dlogT[:, c:c + 1], identc[0:32, 0:32]), waits=[cv], inc="t")
        cv = p.emit("vector", lambda e, c=c: e.tensor_copy(
                    dlogrow[0:1, 32 * c:32 * (c + 1)], PB[6][0:1, 40:72]),
                    waits=[ct], inc="v")
        ct = p.emit("tensor", lambda e, c=c: e.matmul(PB[6][:, 40:72],
                    ones_row[0:1, 0:128], dlogrow[0:1, 32 * c:32 * (c + 1)],
                    start=True, stop=True), waits=[cv], inc="t")
        cv = p.emit("vector", lambda e, c=c: e.tensor_tensor(gst1[:, 0:32],
                    PB[6][:, 40:72], acs[:, 32 * c:32 * (c + 1)], OP.subtract),
                    waits=[ct], inc="v")
        ca = p.emit("scalar", lambda e, c=c: e.activation(ds_tm[:, 32 * c:32 * (c + 1)],
                    gst1[:, 0:32], AF.Exp), waits=[cv], inc="a")

    ca = p.emit("scalar", lambda e: e.activation(gst1[0:32, 0:8], dlogT[:, :], AF.Exp),
                waits=[p.c("v"), p.c("a")], inc="a")

    def mk_pref(e):
        i = e.memset(prefT[:, 0:1], 0.0)
        for c in range(1, NCH):
            p.chain(e, i)
            i = e.tensor_tensor(prefT[:, c:c + 1], prefT[:, c - 1:c],
                                dlogT[:, c - 1:c], OP.add)
        return i
    cv = p.emit("vector", mk_pref, waits=[ca], inc="v")
    ca = p.emit("scalar", lambda e: e.activation(gst1[0:32, 8:16], prefT[:, :], AF.Exp),
                waits=[cv], inc="a")
    for t in range(NPAIR):
        bk = PB[6] if t % 2 == 0 else PB[7]
        ct = p.emit("tensor", lambda e, t=t, bk=bk: e.matmul(bk[:, 0:8],
                    pairsel[:, 128 * t:128 * (t + 1)], gst1[0:32, 0:8],
                    start=True, stop=True), waits=[ca, p.c("v")], inc="t")
        p.emit("vector", lambda e, t=t, bk=bk: e.tensor_copy(dcolp[:, 8 * t:8 * (t + 1)],
               bk[:, 0:8]), waits=[ct], inc="v")
    for t in range(NPAIR):
        bk = PB[6] if t % 2 == 0 else PB[7]
        ct = p.emit("tensor", lambda e, t=t, bk=bk: e.matmul(bk[:, 0:8],
                    pairsel[:, 128 * t:128 * (t + 1)], gst1[0:32, 8:16],
                    start=True, stop=True), waits=[ca, p.c("v")], inc="t")
        p.emit("vector", lambda e, t=t, bk=bk: e.tensor_copy(pdcol[:, 8 * t:8 * (t + 1)],
               bk[:, 0:8]), waits=[ct], inc="v")
    cv_cols = p.c("v")

    if stop_after == "p1a":
        for nm, src, wd in (("acs", acs, NCH * 32), ("ds", ds_tm, NCH * 32),
                            ("eacs", eacs, NCH * 32), ("dcolp", dcolp, 128),
                            ("pdcol", pdcol, 128), ("dlogrow", dlogrow, 256)):
            if nm in taps_out:
                p.dma("gpsimd", taps_out[nm][:, :],
                      src[0:taps_out[nm].shape[0], 0:taps_out[nm].shape[1]],
                      waits=[cv_cols, p.c("a")])
        p.emit("gpsimd", lambda e: e.engine_nop(), waits=p.allc())
        with nc.Block() as block:
            p.build(block)
        return nc, taps_out

    z_tt = {0: None, 1: None}
    z_silu = None
    for j in range(NKC):
        cwv = load_win_tile(6144 + 128 * j, 128)
        for c in range(NCH):
            bank = PB[c % 2]
            idx = 0
            for k in range(NKC):
                for (wt_, xt_) in ((whi, xn_hi), (whi, xn_lo), (wlo, xn_hi)):
                    st, sp = idx == 0, idx == 3 * NKC - 1
                    wv = ()
                    if idx == 0:
                        wv = [cwv] + ([z_tt[c % 2]] if z_tt[c % 2]
                                      else [p.c("v"), p.c("a")])
                    p.emit("tensor", lambda e, k=k, w_=wt_, x_=xt_, c=c, st=st,
                           sp=sp, b=bank:
                           e.matmul(b[:, 0:128],
                                    x_[:, k * TW + HALO + 128 * c:k * TW + HALO + 128 * (c + 1)],
                                    w_[:, 128 * k:128 * (k + 1)], start=st, stop=sp),
                           waits=wv, inc=("t" if sp else None))
                    idx += 1
            ct = p.c("t")
            cv = p.emit("vector", lambda e, j=j, b=bank: e.tensor_tensor(
                        gst2[:, 0:128], b[:, 0:128], zb[:, 128 * j:128 * (j + 1)],
                        OP.add),
                        waits=[ct] + ([z_silu] if z_silu else [p.c("a")]), inc="v")
            z_tt[c % 2] = cv
            z_silu = p.emit("scalar", lambda e, c=c: e.activation(
                            szst[:, 128 * c:128 * (c + 1)], gst2[:, 0:128], AF.Silu),
                            waits=[cv] + ([p.c("qg")] if c == 0 else []), inc="a")
        p.dma("gpsimd", sz_store[j, :, :], szst[:, :], waits=[p.c("a")])
    cqg_sz = p.c("qg")

    if stop_after == "p1z":
        p.emit("gpsimd", lambda e: e.engine_nop(), waits=p.allc())
        with nc.Block() as block:
            p.build(block)
        return nc, taps_out

    ssd_prev = {k: None for k in ("ca_st", "t_T", "cv_t2", "cv_mt", "ca_exp",
                                  "cv_bd", "t_st", "t_yd", "ca_y", "cv_y")}
    prev_evac = None
    prev_ccv = None
    for j in range(NPAIR):
        cwv = load_win_tile(128 * j, 128)
        xtra = ([prev_evac, ssd_prev["cv_mt"]] if prev_evac
                else [p.c("v"), p.c("a")])
        ct = proj_fm(cwv, extra=xtra)
        ca_xev = evac_proj(xbc1, ct, extra=[prev_ccv] if prev_ccv else [])
        cwvB = load_win_tile(2048 + 128 * j, 128)
        cv_xcv = conv_tile(j, xbc1, xbc2, [ca_xev])
        ctB = proj_fm(cwvB, extra=[ca_xev])
        ca_xsil = p.emit("scalar", lambda e: e.activation(xbc2[:, 0:TOK],
                         xbc2[:, 0:TOK], AF.Silu), waits=[cv_xcv], inc="a")
        ca_bev = evac_proj(xbc1, ctB, extra=[cv_xcv])
        cwvC = load_win_tile(4096 + 128 * j, 128)
        cv_bcv = conv_tile(16 + j, xbc1, bsil,
                           [ca_bev] + ([ssd_prev["t_st"]] if ssd_prev["t_st"] else []))
        ctC = proj_fm(cwvC, extra=[ca_bev])
        cv_prevtr = None
        for c in range(NCH):
            wtr = ([ca_xsil, p.c("v")] +
                   ([ssd_prev["ca_st"]] if ssd_prev["ca_st"] else [])) if c == 0                   else [cv_prevtr]
            ctt = p.emit("tensor", lambda e, c=c: e.transpose(PB[3][:, 0:128],
                         xbc2[:, 128 * c:128 * (c + 1)], identc[:, :]),
                         waits=wtr, inc="t")
            cv_prevtr = p.emit("vector", lambda e, c=c: e.tensor_copy(
                               x_tm[:, 128 * c:128 * (c + 1)], PB[3][:, 0:128]),
                               waits=[ctt], inc="v")
        cv_xtm = cv_prevtr
        ca_bsil = p.emit("scalar", lambda e: e.activation(bsil[:, 0:TOK],
                         bsil[:, 0:TOK], AF.Silu), waits=[cv_bcv], inc="a")
        ca_cev = evac_proj(xbc1, ctC, extra=[cv_bcv])
        prev_evac = ca_cev
        cv_ccv = conv_tile(32 + j, xbc1, csil,
                           [ca_cev, p.c("qg")] +
                           ([ssd_prev["t_st"]] if ssd_prev["t_st"] else []))
        prev_ccv = cv_ccv
        ca_csil = p.emit("scalar", lambda e: e.activation(csil[:, 0:TOK],
                         csil[:, 0:TOK], AF.Silu), waits=[cv_ccv], inc="a")
        p.dma("gpsimd", c_store[j, :, :], csil[:, 0:TOK], waits=[ca_csil])

        pv = ssd_prev
        for c in range(NCH):
            ct_ar = p.emit("tensor", lambda e, c=c, j=j: e.transpose(PB[3][0:2, 0:128],
                           acs[:, 32 * c + 2 * j:32 * c + 2 * j + 2], identc[:, :]),
                           waits=(([pv["ca_st"]] if pv["ca_st"] else [ca_csil]) +
                                  ([cv_xtm] if c == 0 else [])),
                           inc="t")
            cv_ar = p.emit("vector", lambda e: e.tensor_copy(arow[:, :], PB[3][0:2, 0:128]),
                           waits=[ct_ar] + ([pv["t_T"]] if pv["t_T"] else []), inc="v")
            p.emit("tensor", lambda e: e.matmul(PB[5][:, 0:128], rowsel[:, 0:128],
                   arow[:, :], start=True, stop=True),
                   waits=[cv_ar] + ([pv["cv_t2"]] if pv["cv_t2"] else []))
            ct_T = p.emit("tensor", lambda e: e.matmul(PB[5][:, 128:256],
                          rowsel[:, 128:256], arow[:, :], start=True, stop=True),
                          inc="t")
            p.emit("tensor", lambda e, c=c: e.matmul(PB[4][:, 0:128],
                   bsil[0:64, 128 * c:128 * (c + 1)], csil[0:64, 128 * c:128 * (c + 1)],
                   start=True, stop=True),
                   waits=[ca_csil] + ([pv["cv_mt"]] if pv["cv_mt"] else []))
            ct_sc = p.emit("tensor", lambda e, c=c: e.matmul(PB[0][:, 0:128],
                           bsil[64:128, 128 * c:128 * (c + 1)],
                           csil[64:128, 128 * c:128 * (c + 1)], start=True, stop=True),
                           inc="t")
            ct_btr = p.emit("tensor", lambda e, c=c: e.transpose(PB[6][:, 0:128],
                            bsil[:, 128 * c:128 * (c + 1)], identc[:, :]),
                            waits=[pv["cv_bd"]] if pv["cv_bd"] else [], inc="t")

            def mk_t2(e, c=c, j=j):
                e.scalar_tensor_tensor(t2s[:, 0:128], PB[5][:, 0:128],
                                       acs[:, 32 * c + 2 * j:32 * c + 2 * j + 1],
                                       maskneg2[:, 0:128], OP.subtract, OP.add)
                return e.scalar_tensor_tensor(t2s[:, 128:256], PB[5][:, 128:256],
                                              acs[:, 32 * c + 2 * j + 1:32 * c + 2 * j + 2],
                                              maskneg2[:, 128:256], OP.subtract, OP.add)
            cv_t2 = p.emit("vector", mk_t2,
                           waits=[ct_T] + ([pv["ca_exp"]] if pv["ca_exp"] else []),
                           inc="v")
            ca_exp = p.emit("scalar", lambda e: e.activation(es[:, :], t2s[:, :], AF.Exp),
                            waits=[cv_t2] + ([pv["cv_mt"]] if pv["cv_mt"] else []),
                            inc="a")

            def mk_mt(e):
                e.tensor_tensor(mts[:, 0:128], es[:, 0:128], PB[4][:, 0:128], OP.mult)
                return e.tensor_tensor(mts[:, 128:256], es[:, 128:256],
                                       PB[0][:, 0:128], OP.mult)
            cv_mt = p.emit("vector", mk_mt,
                           waits=[ca_exp, ct_sc] + ([pv["t_yd"]] if pv["t_yd"] else []),
                           inc="v")

            def mk_bd(e, c=c, j=j):
                e.tensor_scalar(bd_tmS[:, 0:64], PB[6][:, 0:64],
                                ds_tm[:, 32 * c + 2 * j:32 * c + 2 * j + 1], 0.0,
                                OP.mult, OP.add)
                return e.tensor_scalar(bd_tmS[:, 64:128], PB[6][:, 64:128],
                                       ds_tm[:, 32 * c + 2 * j + 1:32 * c + 2 * j + 2],
                                       0.0, OP.mult, OP.add)
            cv_bd = p.emit("vector", mk_bd,
                           waits=[ct_btr] + ([pv["t_st"]] if pv["t_st"] else []),
                           inc="v")
            p.emit("tensor", lambda e, c=c: e.matmul(PB[3][0:64, 0:64],
                   bd_tmS[:, 0:64], x_tm[:, 128 * c:128 * c + 64],
                   start=True, stop=True), waits=[cv_bd, cv_ar])
            ct_st = p.emit("tensor", lambda e, c=c: e.matmul(PB[3][64:128, 0:64],
                           bd_tmS[:, 64:128], x_tm[:, 128 * c + 64:128 * c + 128],
                           start=True, stop=True), inc="t")
            ca_st = p.emit("scalar", lambda e, c=c, j=j: e.activation(
                           states[:, 1024 * c + 64 * j:1024 * c + 64 * (j + 1)],
                           PB[3][:, 0:64], AF.Copy), waits=[ct_st], inc="a")
            p.emit("tensor", lambda e, c=c: e.matmul(PB[7][:, 0:64],
                   mts[:, 0:128], x_tm[:, 128 * c:128 * c + 64],
                   start=True, stop=True),
                   waits=[cv_mt] + ([pv["ca_y"]] if pv["ca_y"] else []))
            ct_yd = p.emit("tensor", lambda e, c=c: e.matmul(PB[7][:, 64:128],
                           mts[:, 128:256], x_tm[:, 128 * c + 64:128 * c + 128],
                           start=True, stop=True), inc="t")
            ca_y = p.emit("scalar", lambda e: e.activation(ytmp[:, :], PB[7][:, 0:128],
                          AF.Copy),
                          waits=[ct_yd] + ([pv["cv_y"]] if pv["cv_y"] else []),
                          inc="a")

            def mk_y(e, c=c, j=j):
                e.scalar_tensor_tensor(ypair[:, 128 * c:128 * c + 64],
                                       x_tm[:, 128 * c:128 * c + 64],
                                       d_rep[:, 2 * j:2 * j + 1], ytmp[:, 0:64],
                                       OP.mult, OP.add)
                return e.scalar_tensor_tensor(ypair[:, 128 * c + 64:128 * c + 128],
                                              x_tm[:, 128 * c + 64:128 * c + 128],
                                              d_rep[:, 2 * j + 1:2 * j + 2],
                                              ytmp[:, 64:128], OP.mult, OP.add)
            cv_y = p.emit("vector", mk_y, waits=[ca_y] + ([p.c("qg")] if c == 0 else []),
                          inc="v")
            pv = {"ca_st": ca_st, "t_T": ct_T, "cv_t2": cv_t2, "cv_mt": cv_mt,
                  "ca_exp": ca_exp, "cv_bd": cv_bd, "t_st": ct_st, "t_yd": ct_yd,
                  "ca_y": ca_y, "cv_y": cv_y}
        ssd_prev = pv
        p.dma("gpsimd", y_store[j, :, :], ypair[:, :], waits=[p.c("v")])
        if stop_after.startswith("p1mt") or stop_after in ("p1pair0", "p1st"):
            break
    ca_p1c = p.c("a")
    if stop_after.startswith("p1mt") or stop_after in ("p1pair0", "p1xt", "p1ct", "p1st"):
        for nm, src_, wd in (("states", states, NCH * 1024), ("ypair", ypair, NCH * 128)):
            if nm in taps_out:
                p.dma("gpsimd", taps_out[nm][:, :], src_[:, 0:wd],
                      waits=[(ca_p1c), p.c("v")])
        p.emit("gpsimd", lambda e: e.engine_nop(), waits=p.allc())
        with nc.Block() as block:
            p.build(block)
        return nc, taps_out

    def scan(e):
        e.memset(gst1[:, 0:1024], 0.0)
        last = None
        for c in range(NCH):
            for t in range(NPAIR):
                sl = slice(1024 * c + 64 * t, 1024 * c + 64 * (t + 1))
                rn = slice(64 * t, 64 * (t + 1))
                e.tensor_copy(gst2[:, 0:64], states[:, sl])
                e.tensor_copy(states[:, sl], gst1[:, rn])
                last = e.scalar_tensor_tensor(gst1[:, rn], gst1[:, rn],
                                              dcolp[:, 8 * t + c:8 * t + c + 1],
                                              gst2[:, 0:64], OP.mult, OP.add)
        return last
    cv_scan = p.emit("vector", scan, waits=[ca_p1c, cv_cols, p.c("qg"), p.c("g")],
                     inc="v")
    if stop_after == "p1x":
        for nm, src_, pr, wd in (("states", states, 128, NCH * 1024),
                                 ("sumout", gst1, 128, 1024)):
            if nm in taps_out:
                p.dma("gpsimd", taps_out[nm][:, :], src_[0:pr, 0:wd],
                      waits=[cv_scan])
        p.emit("gpsimd", lambda e: e.engine_nop(), waits=p.allc())
        with nc.Block() as block:
            p.build(block)
        return nc, taps_out
    cv = p.emit("vector", lambda e: e.tensor_reduce(
                sp_st[0:1, 0:32],
                dlogrow[0:1, 0:256].rearrange("o (c h) -> o h c", h=32),
                axis=AX.X, op=OP.add), waits=[cv_scan], inc="v")
    p.dma("gpsimd", cc_in_t.ap()[0:128, 0:1024], gst1[:, 0:1024], waits=[cv_scan])
    p.dma("gpsimd", cc_in_t.ap()[0:128, 1024:1056], sp_st[:, 0:32], waits=[cv])
    if "dlt9m" in taps_out:
        p.dma("gpsimd", taps_out["dlt9m"][:, :], dlt9[:, :], waits=[p.c("v")])
    cg_ag = p.emit("gpsimd", lambda e: e.collective_compute(
                   "AllGather", OP.bypass, replica_groups=[list(range(NCORES))],
                   ins=[cc_in_t.ap().opt()], outs=[cc_out_t.ap().opt()]),
                   waits=[p.c("qg")], inc="g")

    if stop_after == "p1":
        for nm, src, pr, wd in (("states", states, 128, NCH * 1024),
                                ("sumout", gst1, 128, 1024),
                                ("ccout", cc_out, None, None)):
            if nm in taps_out:
                if nm == "ccout":
                    p.dma("gpsimd", taps_out[nm][:, :], cc_out[:, :],
                          waits=[cg_ag])
                else:
                    p.dma("gpsimd", taps_out[nm][:, :], src[0:pr, 0:wd],
                          waits=[cv_scan, cg_ag])
        p.emit("gpsimd", lambda e: e.engine_nop(), waits=p.allc())
        with nc.Block() as block:
            p.build(block)
        return nc, taps_out

    stacks["s1"].close()
    wq = sbuf("wq", (128, NKC * 128), BF16, st="s2")
    wq2 = sbuf("wq2", (128, NKC * 128), BF16, st="s2")
    wqB = sbuf("wqB", (128, NKC * 128), BF16, st="s2")
    wq2B = sbuf("wq2B", (128, NKC * 128), BF16, st="s2")
    hid_hi = sbuf("hid_hi", (128, JBLK * 1024), BF16, st="s2")
    hid_lo = sbuf("hid_lo", (128, JBLK * 1024), BF16, st="s2")

    jbase = 0
    wpairs = [(wq, wq2), (wqB, wq2B)]
    lastmm = {0: None, 1: None}
    prev_sig = None
    prev_x2f = None
    prev_hid = None
    prev_q = None
    for blk, JB in enumerate(JBLOCKS):
        for jj in range(JB):
            j = jbase + jj
            par = jj % 2
            qd = {}
            for wi, wsrc, dst in ((0, w1T, wpairs[par][0]), (1, w2T, wpairs[par][1])):
                src_ = wsrc.rearrange("(k p) e -> p k e", p=128)[:, :, 128 * j:128 * (j + 1)]
                wl = p.dma("sync", wstage[:, :].rearrange("p (k e) -> p k e", e=128),
                           src_, waits=[prev_q] if prev_q else [p.c("v"), p.c("t")])

                def quant(e, dst=dst, wi=wi):
                    e.tensor_scalar(whi[:, :], wstage[:, :], th_col[:, wi:wi + 1],
                                    0.0, OP.is_ge, OP.add)
                    e.tensor_scalar(wlo[:, :], wstage[:, :],
                                    th_col[:, 3 + wi:4 + wi], -1.0,
                                    OP.is_le, OP.mult)
                    return e.tensor_tensor(dst[:, :], whi[:, :], wlo[:, :], OP.add)
                qd[wi] = p.emit("vector", quant,
                                waits=list(wl) + ([lastmm[par]] if lastmm[par]
                                                  else [p.c("t")]), inc="v")
                prev_q = qd[wi]
            for ts in range(2):
                tok0 = HALO + 512 * ts
                for wi, wqt in ((0, wpairs[par][0]), (1, wpairs[par][1])):
                    idx = 0
                    for k in range(NKC):
                        for xt_ in (xn_hi, xn_lo):
                            st, sp = idx == 0, idx == 2 * NKC - 1
                            wv = []
                            if idx == 0:
                                wv = [qd[wi]]
                                pb_free = prev_sig if wi == 0 else prev_x2f
                                if pb_free:
                                    wv.append(pb_free)
                            p.emit("tensor", lambda e, k=k, x_=xt_, w_=wqt, wi=wi,
                                   st=st, sp=sp, tok0=tok0:
                                   e.matmul(PB[wi][:, 0:512],
                                            w_[:, 128 * k:128 * (k + 1)],
                                            x_[:, k * TW + tok0:k * TW + tok0 + 512],
                                            start=st, stop=sp),
                                   waits=wv, inc=("t" if sp else None))
                            idx += 1
                ct = p.c("t")
                lastmm[par] = ("t", ct[1])
                prev_sig = p.emit("scalar", lambda e: e.activation(gst1[:, 0:512],
                                  PB[0][:, 0:512], AF.Sigmoid),
                                  waits=[ct] + ([prev_hid] if prev_hid else []),
                                  inc="a")
                prev_x2f = p.emit("scalar", lambda e: e.activation(gst2[:, 0:512],
                                  PB[1][:, 0:512], AF.Copy), waits=[prev_sig], inc="a")

                def mk_hid(e, jj=jj, ts=ts):
                    off = 1024 * jj + 512 * ts
                    e.tensor_tensor(gst3[:, 0:512], gst1[:, 0:512], gst2[:, 0:512],
                                    OP.mult)
                    e.tensor_copy(hid_hi[:, off:off + 512], gst3[:, 0:512])
                    return e.scalar_tensor_tensor(hid_lo[:, off:off + 512],
                                                  gst3[:, 0:512], 1.0,
                                                  hid_hi[:, off:off + 512],
                                                  OP.mult, OP.subtract)
                prev_hid = p.emit("vector", mk_hid, waits=[prev_x2f], inc="v")
        cv_hid = p.c("v")
        last3 = {0: None, 1: None}
        prev_q3 = None
        for dq in range(4):
            for jj in range(JB):
                j = jbase + jj
                par = jj % 2
                buf3 = wpairs[par][0]
                src_ = w3T[128 * j:128 * (j + 1), 512 * dq:512 * (dq + 1)]
                wl = p.dma("sync", wstage[:, 0:512], src_,
                           waits=[prev_q3] if prev_q3 else [p.c("v"), p.c("t")])

                def quant3(e, buf3=buf3):
                    e.tensor_scalar(whi[:, 0:512], wstage[:, 0:512], th_col[:, 2:3],
                                    0.0, OP.is_ge, OP.add)
                    e.tensor_scalar(wlo[:, 0:512], wstage[:, 0:512], th_col[:, 5:6],
                                    -1.0, OP.is_le, OP.mult)
                    return e.tensor_tensor(buf3[:, 0:512], whi[:, 0:512],
                                           wlo[:, 0:512], OP.add)
                cvq = p.emit("vector", quant3,
                             waits=list(wl) + [cv_hid] +
                                   ([last3[par]] if last3[par] else [p.c("t")]),
                             inc="v")
                prev_q3 = cvq
                ctl = None
                for d4 in range(4):
                    for ts in range(2):
                        for pi, ht in enumerate((hid_hi, hid_lo)):
                            st = (jj == 0 and pi == 0)
                            sp = (jj == JB - 1 and pi == 1)
                            ctl = p.emit("tensor",
                                   lambda e, d4=d4, ts=ts, h_=ht, jj=jj, st=st, sp=sp,
                                   buf3=buf3:
                                   e.matmul(PB[2 * d4 + ts][:, 0:512],
                                            buf3[:, 128 * d4:128 * (d4 + 1)],
                                            h_[:, 1024 * jj + 512 * ts:1024 * jj + 512 * ts + 512],
                                            start=st, stop=sp),
                                   waits=[cvq, p.c("a")] if (d4 == 0 and ts == 0 and pi == 0 and jj == 0)
                                         else ([cvq] if (d4 == 0 and ts == 0 and pi == 0) else ()),
                                   inc=("t" if (d4 == 3 and ts == 1 and pi == 1) else None))
                last3[par] = ctl
            ct = p.c("t")
            for d4 in range(4):
                d = 4 * dq + d4
                if blk < len(JBLOCKS) - 1:
                    if blk == 0:
                        def ev0(e, d4=d4):
                            e.tensor_copy(gst1[:, 0:512], PB[2 * d4][:, 0:512])
                            return e.tensor_copy(gst1[:, 512:1024],
                                                 PB[2 * d4 + 1][:, 0:512])
                        cv = p.emit("vector", ev0, waits=[ct, p.c("qg")], inc="v")
                    else:
                        wl0 = p.dma("sync", gst2[:, 0:1024], fp_store[d, :, :],
                                    waits=[p.c("v"), p.c("qg")])
                        def ev0b(e, d4=d4):
                            e.tensor_tensor(gst1[:, 0:512], PB[2 * d4][:, 0:512],
                                            gst2[:, 0:512], OP.add)
                            return e.tensor_tensor(gst1[:, 512:1024],
                                                   PB[2 * d4 + 1][:, 0:512],
                                                   gst2[:, 512:1024], OP.add)
                        cv = p.emit("vector", ev0b,
                                    waits=[ct, p.c("qg")] + list(wl0), inc="v")
                    p.dma("gpsimd", fp_store[d, :, :], gst1[:, 0:1024], waits=[cv])
                else:
                    wl = p.dma("sync", gst2[:, 0:1024], fp_store[d, :, :],
                               waits=[p.c("v"), p.c("qg")])

                    def ev1(e, d4=d4):
                        e.tensor_tensor(gst1[:, 0:512], PB[2 * d4][:, 0:512],
                                        gst2[:, 0:512], OP.add)
                        return e.tensor_tensor(gst1[:, 512:1024],
                                               PB[2 * d4 + 1][:, 0:512],
                                               gst2[:, 512:1024], OP.add)
                    cv = p.emit("vector", ev1, waits=[ct] + list(wl), inc="v")
                    ca = p.emit("scalar", lambda e: e.activation(gst3[:, 0:1024],
                                gst1[:, 0:1024], AF.Silu),
                                waits=[cv, p.c("qg")], inc="a")
                    p.dma("gpsimd", g_store[d, :, :], gst3[:, 0:1024], waits=[ca])
            prev_q3 = None
            last3 = {0: None, 1: None}
        jbase += JB
        lastmm = {0: None, 1: None}
        prev_q = None

    if stop_after == "p2":
        p.emit("gpsimd", lambda e: e.engine_nop(), waits=p.allc())
        with nc.Block() as block:
            p.build(block)
        return nc, taps_out

    if "gout" in taps_out:
        p.dma("gpsimd", taps_out["gout"][:, :].rearrange("(j p) f -> j p f", p=128),
              g_store[:, :, :], waits=[p.c("qg"), p.c("a"), p.c("v"), p.c("t")])
    if "hid0" in taps_out:
        p.dma("gpsimd", taps_out["hid0"][:, :], hid_hi[:, 0:1024],
              waits=[p.c("qg"), p.c("a"), p.c("v"), p.c("t")])
    stacks["s2"].close()
    sglob = sbuf("sglob", (128, 1024), st="s3")
    sfull = sbuf("sfull", (128, 1024), st="s3")
    ynh = sbuf("ynh", (128, NKC * 512), BF16, st="s3")
    ynl = sbuf("ynl", (128, NKC * 512), BF16, st="s3")
    ytmp3 = sbuf("ytmp3", (128, 128), st="s3")
    rmsst = sbuf("rmsst", (128, 64), st="s3")

    p.emit("vector", lambda e: e.memset(sglob[:, :], 0.0), waits=[p.c("a"), p.c("t")],
           inc="v")
    wl = p.dma("sync", dlt9[0:8, :],
               cc_out.rearrange("(j p) f -> j p f", p=128)[:, 0:1, 1024:1056],
               waits=[cg_ag, p.c("v")])
    ct = p.emit("tensor", lambda e: e.matmul(PB[6][0:32, 0:8], dlt9[:, :], rhs9s[:, :],
                start=True, stop=True), waits=list(wl) + [p.c("a"), p.c("v")], inc="t")
    ca = p.emit("scalar", lambda e: e.activation(coefT[:, :], PB[6][0:32, 0:8], AF.Exp),
                waits=[ct], inc="a")
    for t in range(NPAIR):
        bk = PB[6] if t % 2 == 0 else PB[7]
        ct = p.emit("tensor", lambda e, t=t, bk=bk: e.matmul(bk[:, 0:8],
                    pairsel[:, 128 * t:128 * (t + 1)], coefT[:, :],
                    start=True, stop=True), waits=[ca, p.c("v"), p.c("a")], inc="t")
        p.emit("vector", lambda e, t=t, bk=bk: e.tensor_copy(coefc[:, 8 * t:8 * (t + 1)],
               bk[:, 0:8]), waits=[ct], inc="v")
    for jcore in range(NCORES):
        wl = p.dma("sync", gst1[:, 0:1024],
                   cc_out.rearrange("(j p) f -> j p f", p=128)[jcore, :, 0:1024],
                   waits=[p.c("v")])

        def acc_sg(e, jcore=jcore):
            last = None
            for t in range(NPAIR):
                last = e.scalar_tensor_tensor(sglob[:, 64 * t:64 * (t + 1)],
                                              gst1[:, 64 * t:64 * (t + 1)],
                                              coefc[:, 8 * t + jcore:8 * t + jcore + 1],
                                              sglob[:, 64 * t:64 * (t + 1)],
                                              OP.mult, OP.add)
            return last
        p.emit("vector", acc_sg, waits=list(wl) + [p.c("v")], inc="v")
    cv_sglob = p.c("v")
    if "sg0" in taps_out:
        p.dma("gpsimd", taps_out["sg0"][:, :], sglob[:, :], waits=[cv_sglob])
    if "coefT" in taps_out:
        p.dma("gpsimd", taps_out["coefT"][:, :], coefT[:, :], waits=[cv_sglob])
    if "coefc" in taps_out:
        p.dma("gpsimd", taps_out["coefc"][:, :], coefc[:, :], waits=[cv_sglob])
    if "dlt9" in taps_out:
        p.dma("gpsimd", taps_out["dlt9"][:, :], dlt9[:, :], waits=[cv_sglob])
    if "sjlast" in taps_out:
        p.dma("gpsimd", taps_out["sjlast"][:, :], gst1[:, 0:1024], waits=[cv_sglob])
    wl_rms = p.dma("sync", zb[:, :], rms_h[:, :], waits=[cv_sglob, p.c("a")])

    op_ev = {0: None, 1: None}
    op_qg = {0: None, 1: None}
    for half in range(2):
        for ci in range(4):
            c = 4 * half + ci

            def mk_sfull(e, c=c):
                last = None
                for t in range(NPAIR):
                    last = e.scalar_tensor_tensor(
                        sfull[:, 64 * t:64 * (t + 1)], sglob[:, 64 * t:64 * (t + 1)],
                        pdcol[:, 8 * t + c:8 * t + c + 1],
                        states[:, 1024 * c + 64 * t:1024 * c + 64 * (t + 1)],
                        OP.mult, OP.add)
                return last
            cv_sf = p.emit("vector", mk_sfull, waits=[cv_sglob, p.c("a"), p.c("t")],
                           inc="v")
            wl_c = p.dma("sync", gst1[:, 0:2048].rearrange("p (j f) -> p j f", f=128),
                         c_store.rearrange("j p f -> p j f")[:, :, 128 * c:128 * (c + 1)],
                         waits=[p.c("v")])
            wl_y = p.dma("sync", gst2[:, 0:2048].rearrange("p (j f) -> p j f", f=128),
                         y_store.rearrange("j p f -> p j f")[:, :, 128 * c:128 * (c + 1)],
                         waits=[p.c("v"), p.c("qg")])
            wl_s = p.dma("sync", gst3[:, 0:2048].rearrange("p (j f) -> p j f", f=128),
                         sz_store.rearrange("j p f -> p j f")[:, :, 128 * c:128 * (c + 1)],
                         waits=[p.c("v"), p.c("qg")])
            for t in range(NPAIR):
                p.emit("tensor", lambda e, t=t: e.matmul(PB[4][:, 0:64],
                       gst1[0:64, 128 * t:128 * (t + 1)],
                       sfull[0:64, 64 * t:64 * (t + 1)], start=True, stop=True),
                       waits=(list(wl_s) + [cv_sf, p.c("a"), p.c("v")]) if t == 0
                             else [p.c("a")])
                ct = p.emit("tensor", lambda e, t=t: e.matmul(PB[5][:, 0:64],
                            gst1[64:128, 128 * t:128 * (t + 1)],
                            sfull[64:128, 64 * t:64 * (t + 1)],
                            start=True, stop=True), inc="t")

                def ev_y(e, t=t, c=c):
                    e.activation(ytmp3[:, 0:64], PB[4][:, 0:64], AF.Copy,
                                 scale=eacs[:, 32 * c + 2 * t:32 * c + 2 * t + 1])
                    return e.activation(ytmp3[:, 64:128], PB[5][:, 0:64], AF.Copy,
                                        scale=eacs[:, 32 * c + 2 * t + 1:32 * c + 2 * t + 2])
                ca = p.emit("scalar", ev_y,
                            waits=[ct] + (list(wl_s) if t == 0 else []),
                            inc="a")

                def mk_ygf(e, t=t):
                    e.tensor_tensor(ytmp3[:, :], ytmp3[:, :],
                                    gst2[:, 128 * t:128 * (t + 1)], OP.add)
                    return e.tensor_tensor(gst2[:, 128 * t:128 * (t + 1)], ytmp3[:, :],
                                           gst3[:, 128 * t:128 * (t + 1)], OP.mult)
                p.emit("vector", mk_ygf, waits=[ca], inc="v")
            cv_ygf = p.c("v")
            if c == 0 and "ygf0" in taps_out:
                p.dma("gpsimd", taps_out["ygf0"][:, :], gst2[:, 0:2048],
                      waits=[p.c("v")])
            ca = p.emit("scalar", lambda e: e.activation(gst1[:, 0:2048],
                        gst2[:, 0:2048], AF.Square), waits=[cv_ygf], inc="a")

            def mk_rms(e):
                i1 = e.tensor_reduce(rmsst[:, 0:1], gst1[:, 0:2048], axis=AX.X,
                                     op=OP.add)
                p.chain(e, i1)
                i2 = e.tensor_scalar(rmsst[:, 1:2], rmsst[:, 0:1], 1.0 / DM, 1e-5,
                                     OP.mult, OP.add)
                p.chain(e, i2)
                return e.reciprocal(rmsst[:, 2:3], rmsst[:, 1:2])
            cv = p.emit("vector", mk_rms, waits=[ca], inc="v")
            ca = p.emit("scalar", lambda e: e.activation(rmsst[:, 3:4], rmsst[:, 2:3],
                        AF.Sqrt), waits=[cv], inc="a")

            def rms_fin(e):
                y = rmsst[:, 3:4]
                i1 = e.tensor_tensor(rmsst[:, 4:5], y, y, OP.mult)
                p.chain(e, i1)
                i2 = e.tensor_tensor(rmsst[:, 4:5], rmsst[:, 4:5], rmsst[:, 1:2],
                                     OP.mult)
                p.chain(e, i2)
                i3 = e.tensor_scalar(rmsst[:, 4:5], rmsst[:, 4:5], -0.5, 1.5,
                                     OP.mult, OP.add)
                p.chain(e, i3)
                i4 = e.tensor_tensor(y, y, rmsst[:, 4:5], OP.mult)
                p.chain(e, i4)
                e.tensor_scalar(gst2[:, 0:2048], gst2[:, 0:2048], y, 0.0,
                                OP.mult, OP.add)
                return e.tensor_tensor(gst2[:, 0:2048], gst2[:, 0:2048], zb[:, :],
                                       OP.mult)
            cv_yn = p.emit("vector", rms_fin, waits=[ca] + list(wl_s), inc="v")
            if c == 0 and "yn0" in taps_out:
                p.dma("gpsimd", taps_out["yn0"][:, :], gst2[:, 0:2048],
                      waits=[cv_yn])
            if c == 0 and "rmsst" in taps_out:
                p.dma("gpsimd", taps_out["rmsst"][:, :], rmsst[:, 0:8],
                      waits=[cv_yn])
            if c == 0 and "sq0" in taps_out:
                p.dma("gpsimd", taps_out["sq0"][:, :], gst1[:, 0:2048],
                      waits=[cv_yn])
            if stop_after == "p3c0":
                for nm, srcb, wd in (("rmsstp", rmsst, 8), ("yn0p", gst2, 2048),
                                     ("sq0p", gst1, 2048), ("ygf0p", None, 0)):
                    if nm in taps_out and srcb is not None:
                        p.dma("gpsimd", taps_out[nm][:, :], srcb[:, 0:wd],
                              waits=[cv_yn])
                p.emit("gpsimd", lambda e: e.engine_nop(), waits=p.allc())
                with nc.Block() as block:
                    p.build(block)
                return nc, taps_out
            for q4 in range(4):
                for ti in range(4):
                    e_t = 4 * q4 + ti
                    p.emit("tensor", lambda e, e_t=e_t, ti=ti: e.transpose(
                           PB[5][:, 128 * ti:128 * (ti + 1)],
                           gst2[:, 128 * e_t:128 * (e_t + 1)], identc[:, :]),
                           waits=[cv_yn, p.c("v"), p.c("a")] if ti == 0 else (),
                           inc=("t" if ti == 3 else None))
                ct = p.c("t")

                def ev_yn(e, q4=q4, ci=ci):
                    last = None
                    for ti in range(4):
                        e_t = 4 * q4 + ti
                        dsth = ynh[:, 512 * e_t + 128 * ci:512 * e_t + 128 * (ci + 1)]
                        dstl = ynl[:, 512 * e_t + 128 * ci:512 * e_t + 128 * (ci + 1)]
                        e.tensor_copy(dsth, PB[5][:, 128 * ti:128 * (ti + 1)])
                        last = e.scalar_tensor_tensor(
                            dstl, PB[5][:, 128 * ti:128 * (ti + 1)], 1.0, dsth,
                            OP.mult, OP.subtract)
                    return last
                p.emit("vector", ev_yn, waits=[ct], inc="v")
        cv_yn_half = p.c("v")
        for dq in range(2):
            for et in range(NKC):
                src = w_outT.rearrange("(k p) d -> p k d", p=128)[
                    :, et:et + 1, 1024 * dq:1024 * (dq + 1)]
                w = list(lw_state["spl"]) if lw_state["spl"] else [p.c("t")]
                wl = p.dma("sync", wstage[:, 0:1024].rearrange("p (k d) -> p k d", d=1024),
                           src, waits=w)
                ca = p.emit("scalar", lambda e: e.activation(whi[:, 0:1024],
                            wstage[:, 0:1024], AF.Copy),
                            waits=list(wl) + [p.c("t")], inc="a")
                cv = p.emit("vector", lambda e: e.scalar_tensor_tensor(wlo[:, 0:1024],
                            wstage[:, 0:1024], 1.0, whi[:, 0:1024], OP.mult,
                            OP.subtract), waits=[ca], inc="v")
                lw_state["spl"] = (ca, cv)
                for d8 in range(8):
                    for (wt_, yt_) in ((whi, ynh), (whi, ynl), (wlo, ynh)):
                        st = (et == 0 and yt_ is ynh and wt_ is whi)
                        sp = (et == NKC - 1 and wt_ is wlo)
                        p.emit("tensor", lambda e, d8=d8, w_=wt_, y_=yt_, et=et,
                               st=st, sp=sp:
                               e.matmul(PB[d8][:, 0:512],
                                        w_[:, 128 * d8:128 * (d8 + 1)],
                                        y_[:, 512 * et:512 * (et + 1)],
                                        start=st, stop=sp),
                               waits=[cv, cv_yn_half, p.c("a")] if (d8 == 0 and wt_ is whi and yt_ is ynh and et == 0) else ([cv] if (d8 == 0 and wt_ is whi and yt_ is ynh) else ()),
                               inc=("t" if (d8 == 7 and wt_ is wlo) else None))
            ct = p.c("t")
            for d8 in range(8):
                d = 8 * dq + d8
                par = d8 % 2
                stg = gst1 if par == 0 else gst3
                ostg = gst2[:, 0:512] if par == 0 else gst2[:, 512:1024]
                wld = ([op_ev[par]] if op_ev[par] else [p.c("v"), p.c("qg")])
                p.dma("sync", stg[:, 0:512],
                      g_store[d, :, 512 * half:512 * (half + 1)],
                      waits=wld, sem=f"b{par}")
                wl_u = p.dma("sync", stg[:, 512:1024],
                             u_fm[128 * d:128 * (d + 1),
                                  HALO + 512 * half:HALO + 512 * (half + 1)],
                             waits=[], sem=f"b{par}")

                def ev_out(e, d8=d8, stg=stg, ostg=ostg):
                    e.tensor_tensor(ostg, PB[d8][:, 0:512], stg[:, 0:512], OP.mult)
                    return e.tensor_tensor(ostg, ostg, stg[:, 512:1024], OP.add)
                cv = p.emit("vector", ev_out,
                            waits=[ct] + list(wl_u) +
                                  ([op_qg[par]] if op_qg[par] else [p.c("qg")]),
                            inc="v")
                p.dma("gpsimd", out_fm[128 * d:128 * (d + 1),
                                       512 * half:512 * (half + 1)],
                      ostg, waits=[cv])
                op_ev[par] = cv
                op_qg[par] = p.c("qg")

    p.emit("gpsimd", lambda e: e.engine_nop(), waits=p.allc())
    with nc.Block() as block:
        p.build(block)
    return nc, taps_out


def _host_prep(inputs):
    u = np.ascontiguousarray(np.asarray(inputs["L0_attn_u"], np.float32)[0])
    uT = np.ascontiguousarray(u.T)
    w_inT = np.ascontiguousarray(np.asarray(inputs["L0_attn_W_in"], np.float32).T)
    w_outT = np.ascontiguousarray(np.asarray(inputs["L0_attn_W_out"], np.float32).T)
    w1T = np.ascontiguousarray(np.asarray(inputs["L0_ffn_w1"], np.float32).T)
    w2T = np.ascontiguousarray(np.asarray(inputs["L0_ffn_w2"], np.float32).T)
    w3T = np.ascontiguousarray(np.asarray(inputs["L0_ffn_w3"], np.float32).T)
    conv_w = np.ascontiguousarray(np.asarray(inputs["L0_attn_conv_w"], np.float32))
    conv_b = np.ascontiguousarray(np.asarray(inputs["L0_attn_conv_b"], np.float32))[:, None]
    zb = np.ascontiguousarray(np.broadcast_to(
        np.asarray(inputs["L0_attn_z_bias"], np.float32), (128, DM)))
    rms = np.ascontiguousarray(np.broadcast_to(
        np.asarray(inputs["L0_attn_rms_w"], np.float32), (128, DM)))
    drep = np.ascontiguousarray(np.broadcast_to(
        np.asarray(inputs["L0_attn_D"], np.float32), (128, NH)))
    consts = host_constants()
    in_maps = []
    for i in range(NCORES):
        t0 = i * TOK
        ufm = np.zeros((DM, TW), np.float32)
        lo = max(0, t0 - HALO)
        ufm[:, HALO - (t0 - lo):] = uT[:, lo:t0 + TOK]
        m = dict(u_fm=ufm, w_inT=w_inT, w_outT=w_outT, w1T=w1T, w2T=w2T, w3T=w3T,
                 conv_w=conv_w, conv_b=conv_b, z_bias_b=zb, rms_b=rms, d_rep=drep,
                 rhs9=per_core_rhs9(i), **consts)
        in_maps.append(m)
    return in_maps


def kernel(**inputs):
    in_maps = _host_prep(inputs)
    nc, _ = build_program("full")
    res = run_bass_kernel_spmd(nc, in_maps, core_ids=list(range(NCORES)))
    out = np.concatenate([r["out_fm"] for r in res.results], axis=1)
    return np.ascontiguousarray(out.T)[None].astype(np.float32)
